# version 07
# baseline (speedup 1.0000x reference)
import numpy as np
import ml_dtypes
import concourse.bass as bass
import concourse.mybir as mybir
from concourse.bass_utils import run_bass_kernel_spmd

F32 = mybir.dt.float32
BF16 = mybir.dt.bfloat16
AF = mybir.ActivationFunctionType
OP = mybir.AluOpType
KB = 1024


class Cfg:
    def __init__(self, D=4096, T=4096, HID=11008, NH=32, NKV=8):
        self.D, self.T, self.HID, self.NH, self.NKV = D, T, HID, NH, NKV
        self.HD = 128
        self.theta = 10000.0
        self.eps = 1e-5
        self.lora_scale = 2.0
        self.NDC = D // 128
        self.NJ = HID // 128
        self.NKB = T // 128
        self.XGN = T // 512
        self.QGN = T // 512
        self.QH = NH // 4
        self.KVH = NKV // 4
        self.TOWN = T // 4
        self.TGS = min(512, self.TOWN)
        self.NTG = self.TOWN // self.TGS
        self.MH = self.NDC // 2
        self.G1 = [list(range(s, min(s + 8, self.NJ))) for s in range(0, self.NJ, 8)]
        self.G2 = [list(range(s, min(s + 4, self.NJ))) for s in range(0, self.NJ, 4)]
        self.scaling = self.HD ** (-0.5)


def _sem(nc, name):
    return [nc.alloc_semaphore(name), 0]


def _inc(sem, ins, k=1):
    ins.then_inc(sem[0], k)
    sem[1] += k
    return (sem[0], sem[1])


def _tok(sem):
    return (sem[0], sem[1])


_WAITED = {}


def _wait(eng, *toks):
    seen = _WAITED.setdefault(id(eng), {})
    for tok in toks:
        if tok is None:
            continue
        key = tok[0].num if hasattr(tok[0], "num") else id(tok[0])
        if seen.get(key, -1) >= tok[1]:
            continue
        seen[key] = tok[1]
        eng.wait_ge(tok[0], tok[1])


class EW:

    def __init__(self, eng, sem):
        self._e = eng
        self._s = sem

    def __getattr__(self, name):
        f = getattr(self._e, name)

        def g(*a, **k):
            self._e.wait_ge(self._s[0], self._s[1])
            return _inc(self._s, f(*a, **k))

        return g


def build_nc(c: Cfg, cutoff=None):
    _WAITED.clear()
    import os
    cutoff = cutoff or os.environ.get("KK_CUTOFF")
    nc = bass.Bass("TRN2", target_bir_lowering=False)
    T, D, NDC, NJ, QH = c.T, c.D, c.NDC, c.NJ, c.QH
    TOWN, TGS, NTG = c.TOWN, c.TGS, c.NTG
    NT512 = T // 512

    xw = nc.dram_tensor("xw", [128, c.XGN, NDC, 512], F32, kind="ExternalInput").ap()
    xro = nc.dram_tensor("xro", [128, NDC, TOWN], F32, kind="ExternalInput").ap()
    wkv = nc.dram_tensor("wkv", [128, 2 * c.KVH, NDC, 128], F32,
                         kind="ExternalInput").ap()
    wq = nc.dram_tensor("wq", [128, QH, NDC, 128], F32, kind="ExternalInput").ap()
    wo = nc.dram_tensor("wo", [NDC, 128, QH, 128], F32, kind="ExternalInput").ap()
    w1b = nc.dram_tensor("w1b", [NJ, 128, NDC, 128], F32, kind="ExternalInput").ap()
    w3b = nc.dram_tensor("w3b", [NJ, 128, NDC, 128], F32, kind="ExternalInput").ap()
    w2b = nc.dram_tensor("w2b", [NJ, 128, NDC, 128], F32, kind="ExternalInput").ap()
    a1t = nc.dram_tensor("a1t", [128, NDC, 16], F32, kind="ExternalInput").ap()
    a3t = nc.dram_tensor("a3t", [128, NDC, 16], F32, kind="ExternalInput").ap()
    a2t = nc.dram_tensor("a2t", [128, NJ, 16], F32, kind="ExternalInput").ap()
    b1t = nc.dram_tensor("b1t", [NJ, 16, 128], F32, kind="ExternalInput").ap()
    b3t = nc.dram_tensor("b3t", [NJ, 16, 128], F32, kind="ExternalInput").ap()
    b2t = nc.dram_tensor("b2t", [16, NDC, 128], F32, kind="ExternalInput").ap()
    cs = nc.dram_tensor("cs", [128, 2, T], BF16, kind="ExternalInput").ap()
    consts = nc.dram_tensor("consts", [128, 520], F32, kind="ExternalInput").ap()
    out_fm = nc.dram_tensor("out_fm", [128, NDC, TOWN], F32,
                            kind="ExternalOutput").ap()

    NBAND = 4 if NDC % 4 == 0 else 1
    DB = D // NBAND
    part_drams = [nc.dram_tensor(f"part_dram{b}", [4 * DB, TOWN], F32,
                                 kind="Internal").ap() for b in range(NBAND)]
    ao_drams = [nc.dram_tensor(f"ao_dram{b}", [DB, TOWN], F32,
                               kind="Internal").ap() for b in range(NBAND)]
    g_dram = nc.dram_tensor("g_dram", [NJ, 128, TOWN], BF16, kind="Internal").ap()
    x1_dram = nc.dram_tensor("x1_dram", [128, NDC, TOWN], BF16, kind="Internal").ap()
    x16_dram = nc.dram_tensor("x16_dram", [128, c.XGN, NDC, 512], BF16,
                              kind="Internal").ap()

    B0 = (nc.sbuf_base + 127) & ~127
    assert B0 + 207 * KB <= nc.sbuf_top, (B0, nc.sbuf_top)
    RA, RB, RC, RD, RE, RF, RG = (0, 64 * KB, 128 * KB, 160 * KB, 176 * KB,
                                  192 * KB, 203 * KB)

    def sb(name, shape, dtype, off):
        return nc.alloc_sbuf_tensor_at(name, shape, dtype, offset=B0 + off)

    mask_sb = sb("mask_sb", [128, 128], F32, RG)
    rt_sb = sb("rt_sb", [128, 128], BF16, RG + 512)
    id_sb = sb("id_sb", [128, 128], BF16, RG + 768)
    ones128 = sb("ones128", [128, 1], BF16, RG + 1024)
    ones128f = sb("ones128f", [128, 1], F32, RG + 1056)
    ones1f = sb("ones1f", [1, 128], F32, RG + 1088)
    csc_sb = sb("csc_sb", [128, 6], F32, RG + 1600)

    sems = {}
    for name in ["c_ld", "c_ldg", "cs0", "cs1", "x16a", "x16b", "xs0", "xs1", "wo0", "wo1", "wq0", "wq1", "wkv2", "w1a0", "w1a1", "w3a0", "w3a1", "xa0", "xa1", "wld0", "wld1", "pe", "act", "dve", "pool",
                 "st0", "st1", "cc", "g_st0", "g_st1", "x1_st",
                 "x1_ld0", "x1_ld1", "o_st0", "o_st1", "xr0", "xr1", "gl0", "gl1",
                 "w2ld0", "w2ld1", "ao0", "ao1", "misc", "p_st0", "p_st1"]:
        sems[name] = _sem(nc, name)
    S = sems

    TE, VE, SE, GE, SY = nc.tensor, nc.vector, nc.scalar, nc.gpsimd, nc.sync
    VW = EW(VE, S["dve"])
    SW = EW(SE, S["act"])
    GW = EW(GE, S["pool"])

    _inc(S["c_ld"], SY.dma_start(out=mask_sb[:], in_=consts[:, 0:128]), 16)
    _inc(S["c_ldg"], GE.dma_start(out=rt_sb[:], in_=consts[:, 128:256]), 16)
    _inc(S["c_ldg"], GE.dma_start(out=id_sb[:], in_=consts[:, 256:384]), 16)
    _inc(S["c_ldg"], GE.dma_start(out=ones128[:], in_=consts[:, 384:385],
                                  allow_slow_non_contiguous=True), 16)
    _inc(S["c_ld"], SY.dma_start(out=ones128f[:], in_=consts[:, 385:386],
                                 allow_slow_non_contiguous=True), 16)
    _inc(S["c_ld"], SY.dma_start(out=ones1f[:], in_=consts[0:1, 386:514]), 16)
    _inc(S["c_ld"], SY.dma_start(out=csc_sb[:], in_=consts[:, 514:520]), 16)
    for _i, _v in enumerate([c.eps, 0.5, -0.5, c.scaling, c.lora_scale, 1.0]):
        nc.const_aps.register(F32, float(_v), csc_sb[:, _i:_i + 1])

    K_sb = sb("K_sb", [128, c.KVH * T], BF16, RD)
    V_tm = sb("V_tm", [128, c.KVH * T], BF16, RE)
    Q_sb = sb("Q_sb", [128, QH * T], BF16, RA)
    y_sb = sb("y_sb", [128, QH * T], BF16, RB)
    c_ld_tok = (_tok(S["c_ld"]), _tok(S["c_ldg"]))

    t1_sb = sb("t1_sb", [128, 512], F32, RF)
    t2_sb = sb("t2_sb", [128, 512], F32, RF + 2 * KB)
    kt_sb = sb("kt_sb", [128, 2 * 512], BF16, RF + 4 * KB)
    vtmp = sb("vtmp", [128, 512], BF16, RF + 6 * KB)
    cs_t = sb("cs_t", [128, 2 * 2 * 512], BF16, RF + 7 * KB)

    HDC = NDC // 2
    hseq = [(xg, hf) for xg in range(c.XGN) for hf in range(2)]

    def proj_phase(out_buf, w_sb, w_tok, psum, nprj, rope, first_units, dest_base,
                   scale, xa_sb, rst, from_x16=False, write_x16=False):
        xa_free = rst.setdefault("xa_free", [[], []])
        prot_free = rst.setdefault("prot_free", [None])
        kt_free = rst.setdefault("kt_free", [[None, None], [None, None]])
        vtmp_free = rst.setdefault("vtmp_free", [None])
        pvt_free = rst.setdefault("pvt_free", [None])
        psum_free = [None]
        cs_free = rst.setdefault("cs_free", [None, None])
        cs_toks = rst.setdefault("cs_toks", [None, None])
        prot = rope["prot"]
        pvt = rope.get("pvt")
        for si, (xg, hf) in enumerate(hseq):
            par = si % 2
            xgp = xg % 2
            x16tok = None
            if hf == 0:
                _wait(SY, cs_free[xgp])
                ld = SY.dma_start(
                    out=cs_t[:, xgp * 1024:(xgp + 1) * 1024]
                    .rearrange("p (a t) -> p a t", a=2),
                    in_=cs[:, :, xg * 512:(xg + 1) * 512])
                cs_toks[xgp] = _inc(S["cs0" if xgp == 0 else "cs1"], ld, 16)
            if from_x16:
                _wait(SY, *xa_free[par])
                ld = SY.dma_start(
                    out=xa_sb[:, par * HDC * 512:(par + 1) * HDC * 512]
                    .rearrange("p (d t) -> p d t", d=HDC),
                    in_=x16_dram[:, xg, hf * HDC:(hf + 1) * HDC, :])
                xtok = _inc(S["xs0" if par == 0 else "xs1"], ld, 16)
            else:
                _wait(GE, *xa_free[par])
                ld = GE.dma_start(
                    out=xa_sb[:, par * HDC * 512:(par + 1) * HDC * 512]
                    .rearrange("p (d t) -> p d t", d=HDC),
                    in_=xw[:, xg, hf * HDC:(hf + 1) * HDC, :])
                xtok = _inc(S["xa0" if par == 0 else "xa1"], ld, 16)
            if write_x16:
                _wait(SY, xtok)
                st = SY.dma_start(
                    out=x16_dram[:, xg, hf * HDC:(hf + 1) * HDC, :],
                    in_=xa_sb[:, par * HDC * 512:(par + 1) * HDC * 512]
                    .rearrange("p (d t) -> p d t", d=HDC))
                x16tok = _inc(S["x16a" if par == 0 else "x16b"], st, 16)
            _wait(TE, xtok)
            if si == 0:
                _wait(TE, *first_units)
            if hf == 0:
                _wait(TE, psum_free[0])
            for ci in range(nprj):
                if si == 0:
                    _wait(TE, w_tok[ci] if isinstance(w_tok, list) else w_tok)
                for dd in range(HDC):
                    ins = TE.matmul(
                        psum[:, ci * 512:(ci + 1) * 512],
                        w_sb[:, (ci * NDC + hf * HDC + dd) * 128:
                             (ci * NDC + hf * HDC + dd) * 128 + 128],
                        xa_sb[:, (par * HDC + dd) * 512:(par * HDC + dd) * 512 + 512],
                        start=(hf == 0 and dd == 0),
                        stop=(hf == 1 and dd == HDC - 1), skip_group_check=True)
            pe_tok = _inc(S["pe"], ins)
            xa_free[par] = [pe_tok, x16tok] if write_x16 else [pe_tok]
            if hf != 1:
                continue
            pe_done = _tok(S["pe"])
            t0 = xg * 512
            last_act = None
            for ci in range(nprj):
                is_v = rope["is_v"](ci)
                if not is_v:
                    kpar = ci % 2
                    _wait(SE, pe_done, kt_free[kpar][0], kt_free[kpar][1])
                    atok = SW.activation(out=kt_sb[:, kpar * 512:(kpar + 1) * 512],
                                         in_=psum[:, ci * 512:(ci + 1) * 512],
                                         func=AF.Copy, scale=scale)
                    last_act = atok
                    _wait(TE, atok, prot_free[0])
                    ins = TE.matmul(prot[:], rt_sb[:],
                                    kt_sb[:, kpar * 512:(kpar + 1) * 512],
                                    start=True, stop=True)
                    rtok = _inc(S["pe"], ins)
                    kt_free[kpar][1] = rtok
                    _wait(VE, cs_toks[xgp], atok)
                    VW.tensor_mul(t1_sb[:], kt_sb[:, kpar * 512:(kpar + 1) * 512],
                                  cs_t[:, xgp * 1024:xgp * 1024 + 512])
                    _wait(VE, rtok)
                    VW.tensor_mul(t2_sb[:], prot[:],
                                  cs_t[:, xgp * 1024 + 512:xgp * 1024 + 1024])
                    dtok = VW.tensor_add(
                        out_buf[:, rope["dest"](ci) * T + t0:
                                rope["dest"](ci) * T + t0 + 512],
                        t1_sb[:], t2_sb[:])
                    kt_free[kpar][0] = dtok
                    prot_free[0] = dtok
                    cs_free[xgp] = dtok
                else:
                    _wait(SE, pe_done, vtmp_free[0])
                    atok = SW.activation(out=vtmp[:], in_=psum[:, ci * 512:(ci + 1) * 512],
                                         func=AF.Copy)
                    last_act = atok
                    _wait(TE, atok, pvt_free[0])
                    for tt in range(4):
                        ins = TE.transpose(pvt[:, tt * 128:(tt + 1) * 128],
                                           vtmp[:, tt * 128:(tt + 1) * 128], id_sb[:])
                    ttok = _inc(S["pe"], ins)
                    vtmp_free[0] = ttok
                    _wait(VE, ttok)
                    kb0 = t0 // 128
                    ci_v = rope["dest"](ci)
                    for tt in range(4):
                        cptok = VW.tensor_copy(
                            V_tm[:, (ci_v * c.NKB + kb0 + tt) * 128:
                                 (ci_v * c.NKB + kb0 + tt) * 128 + 128],
                            pvt[:, tt * 128:(tt + 1) * 128])
                    pvt_free[0] = cptok
            psum_free[0] = last_act

    wkv_sb = sb("wkv_sb", [128, 4 * NDC * 128], BF16, RB)
    wqh0 = sb("wqh0", [128, 4 * NDC * 128], BF16, RB + 4 * NDC * 128 * 2)
    wqh1 = sb("wqh1", [128, 4 * NDC * 128], BF16, RB)
    xa_sb = sb("xa_sb", [128, 2 * HDC * 512], BF16, RC)
    wq_toks = [None, None]
    ld = GE.dma_start(
        out=wqh0[:].rearrange("p (q d m) -> p q d m", q=4, d=NDC),
        in_=wq[:, 0:4, :, :])
    wq_toks[0] = _inc(S["wq0"], ld, 16)
    with (
        nc.psum_tensor("pkv", [128, 4 * 512], F32) as pkv,
        nc.psum_tensor("prot1", [128, 512], F32) as prot1,
        nc.psum_tensor("pvt1", [128, 1024], BF16) as pvt1,
    ):
        ld = GE.dma_start(
            out=wkv_sb[:, 0:2 * NDC * 128]
            .rearrange("p (cc d m) -> p cc d m", cc=2, d=NDC),
            in_=wkv[:, 0:2, :, :])
        wkv_tok_k = _inc(S["wld0"], ld, 16)
        ld = GE.dma_start(
            out=wkv_sb[:, 2 * NDC * 128:4 * NDC * 128]
            .rearrange("p (cc d m) -> p cc d m", cc=2, d=NDC),
            in_=wkv[:, 2:4, :, :])
        wkv_tok = _inc(S["wkv2"], ld, 16)
        proj_phase(K_sb, wkv_sb,
                   [wkv_tok_k, wkv_tok_k, wkv_tok, wkv_tok], pkv, 4,
                   dict(prot=prot1, pvt=pvt1,
                        is_v=lambda ci: ci >= 2, dest=lambda ci: ci % 2),
                   list(c_ld_tok), 0, 1.0, xa_sb, {}, write_x16=True)

    nc.all_engine_barrier()

    if cutoff == "p1a":
        SY.wait_ge(S["o_st0"][0], 0)
        return nc
    SY.wait_ge(S["x16a"][0], S["x16a"][1])
    SY.wait_ge(S["x16b"][0], S["x16b"][1])
    with (
        nc.psum_tensor("pq", [128, 4 * 512], F32) as pq,
        nc.psum_tensor("prot2", [128, 512], F32) as prot2,
    ):
        ld = GE.dma_start(
            out=wqh1[:].rearrange("p (q d m) -> p q d m", q=4, d=NDC),
            in_=wq[:, 4:8, :, :])
        wq_toks[1] = _inc(S["wq1"], ld, 16)
        rst = {}
        for half in range(2):
            proj_phase(Q_sb, (wqh0 if half == 0 else wqh1)[:],
                       wq_toks[half], pq, 4,
                       dict(prot=prot2, pvt=None,
                            is_v=lambda ci: False,
                            dest=lambda ci, _h=half: _h * 4 + ci),
                       [], 0, float(c.scaling), xa_sb, rst, from_x16=True)

    nc.all_engine_barrier()

    if cutoff == "p1b":
        SY.wait_ge(S["o_st0"][0], 0)
        return nc
    sumacc = sb("sumacc", [128, 2 * 512], BF16, RC)
    rb_sb = sb("rb_sb", [128, 2 * 512], F32, RC + 3 * KB)
    rinv = sb("rinv", [1, 2 * 512], F32, RC + 7 * KB)
    bias_h = sb("bias_h", [128, 2], F32, RC + 11 * KB)
    nrm = sb("nrm", [1, 64], F32, RC + 11 * KB + 64)
    sqt = sb("sqt", [128, 512], BF16, RC + 12 * KB)
    Ppt = sb("Ppt", [128, 4 * 512], BF16, RC + 13 * KB)

    with (
        nc.psum_tensor("pS", [128, 4 * 512], F32) as pS,
        nc.psum_tensor("pO", [128, 2 * 512], F32) as pO,
        nc.psum_tensor("pden", [128, 512], F32) as pden,
        nc.psum_tensor("psml", [128, 512], F32) as psml,
    ):
        sq_free = None
        psml_free = None
        for h in range(c.KVH):
            slot = 0
            for kind, idx in [("q", 4 * h + gq) for gq in range(4)] + [("k", h)]:
                buf = Q_sb if kind == "q" else K_sb
                for tgi in range(NT512):
                    _wait(SE, sq_free)
                    atok = SW.activation(out=sqt[:],
                                         in_=buf[:, idx * T + tgi * 512:
                                                 idx * T + tgi * 512 + 512],
                                         func=AF.Square)
                    _wait(TE, atok, psml_free)
                    ins = TE.matmul(psml[:1, :], ones128[:], sqt[:],
                                    start=True, stop=True, skip_group_check=True)
                    mtok = _inc(S["pe"], ins)
                    sq_free = mtok
                    _wait(VE, mtok)
                    psml_free = VW.reduce_max(nrm[:, slot:slot + 1], psml[:1, :],
                                              axis=mybir.AxisListType.X)
                    slot += 1
            nq = 4 * NT512
            VW.reduce_max(nrm[:, 60:61], nrm[:, 0:nq], axis=mybir.AxisListType.X)
            VW.reduce_max(nrm[:, 61:62], nrm[:, nq:slot], axis=mybir.AxisListType.X)
            vtok = VW.tensor_mul(nrm[:, 62:63], nrm[:, 60:61], nrm[:, 61:62])
            _wait(SE, vtok)
            SW.activation(out=nrm[:, 63:64], in_=nrm[:, 62:63], func=AF.Ln)
            atok = SW.activation(out=nrm[:, 62:63], in_=nrm[:, 63:64], func=AF.Exp,
                                 scale=0.5)
            _wait(VE, atok)
            vtok = VW.tensor_scalar_mul(nrm[:, 63:64], nrm[:, 62:63], -1.0)
            _wait(TE, vtok, psml_free)
            ins = TE.matmul(psml[:, 0:1], ones1f[:], nrm[:, 63:64],
                            start=True, stop=True, skip_group_check=True)
            mtok = _inc(S["pe"], ins)
            sq_free = mtok
            _wait(VE, mtok)
            psml_free = VW.tensor_copy(bias_h[:, h:h + 1], psml[:, 0:1])
        nc.all_engine_barrier()

        pS_free = [None] * 4
        PptAV_free = [None] * 4
        PptSum_free = [None] * 4
        pO_free = [None, None]
        sum_free = [None, None]
        pden_free = [None, None]
        rinv_free = [None, None]
        mtok_mask = None
        pend = [None]
        gctr = [0]

        def emit_epi1():
            if pend[0] is None or pend[0].get("s1"):
                return
            e = pend[0]
            e["s1"] = True
            epar_ = e["epar"]
            pd = pden if epar_ == 0 else psml
            _wait(TE, e["sum_tok"], pden_free[epar_])
            ins = TE.matmul(pd[:1, :], ones128[:],
                            sumacc[:, epar_ * 512:(epar_ + 1) * 512],
                            start=True, stop=True, skip_group_check=True)
            dtok = _inc(S["pe"], ins)
            sum_free[epar_] = dtok
            _wait(VE, dtok, rinv_free[epar_])
            e["rtok"] = VW.reciprocal(rinv[:, epar_ * 512:(epar_ + 1) * 512],
                                      pd[:1, :])

        def emit_epi2():
            if pend[0] is None:
                return
            emit_epi1()
            e = pend[0]
            pend[0] = None
            epar_ = e["epar"]
            qpar_ = e["qpar"]
            pd = pden if epar_ == 0 else psml
            _wait(TE, e["rtok"])
            ins = TE.matmul(pd[:, :], ones1f[:],
                            rinv[:, epar_ * 512:(epar_ + 1) * 512],
                            start=True, stop=True, skip_group_check=True)
            btok = _inc(S["pe"], ins)
            rinv_free[epar_] = btok
            _wait(VE, btok)
            pden_free[epar_] = VW.tensor_copy(
                rb_sb[:, epar_ * 512:(epar_ + 1) * 512], pd[:, :])
            _wait(VE, e["av_done"])
            pO_free[qpar_] = VW.tensor_mul(
                y_sb[:, e["i"] * T + e["qg"] * 512:e["i"] * T + (e["qg"] + 1) * 512],
                pO[:, qpar_ * 512:(qpar_ + 1) * 512],
                rb_sb[:, epar_ * 512:(epar_ + 1) * 512])

        for h in range(c.KVH):
            for g in range(4):
                i = 4 * h + g
                for qg in range(c.QGN):
                    qpar = qg % 2
                    epar = gctr[0] % 2
                    gctr[0] += 1
                    nkb = 4 * (qg + 1)
                    _wait(VE, sum_free[epar])
                    VW.memset(sumacc[:, epar * 512:(epar + 1) * 512], 0.0)
                    exp_toks = [None] * nkb
                    q0s = [max(0, kb * 128 - qg * 512) for kb in range(nkb)]

                    def emit_av(kb, _nkb=None, _qpar=None, _h=None, _exp=None):
                        par = kb % 4
                        q0 = q0s[kb]
                        _wait(TE, exp_toks[kb])
                        if kb == 0:
                            _wait(TE, pO_free[qpar])
                        ins = TE.matmul(
                            pO[:, qpar * 512 + q0:qpar * 512 + 512],
                            V_tm[:, (h * c.NKB + kb) * 128:
                                 (h * c.NKB + kb) * 128 + 128],
                            Ppt[:, par * 512 + q0:par * 512 + 512],
                            start=(kb == 0), stop=(kb == nkb - 1),
                            skip_group_check=True)
                        PptAV_free[par] = _inc(S["pe"], ins)

                    for kb in range(nkb):
                        par = kb % 4
                        q0 = q0s[kb]
                        _wait(TE, pS_free[par])
                        ins = TE.matmul(
                            pS[:, par * 512 + q0:par * 512 + 512],
                            K_sb[:, h * T + kb * 128:h * T + kb * 128 + 128],
                            Q_sb[:, i * T + qg * 512 + q0:i * T + (qg + 1) * 512],
                            start=True, stop=True, skip_group_check=True)
                        qk_tok = _inc(S["pe"], ins)
                        if kb == 1:
                            emit_epi1()
                        elif kb == 3:
                            emit_epi2()
                        diag = kb >= qg * 4
                        if diag:
                            d0 = kb * 128 - qg * 512
                            _wait(VE, qk_tok)
                            mtok_mask = VW.tensor_add(
                                pS[:, par * 512 + d0:par * 512 + d0 + 128],
                                pS[:, par * 512 + d0:par * 512 + d0 + 128],
                                mask_sb[:])
                        _wait(SE, mtok_mask if diag else qk_tok,
                              PptAV_free[par], PptSum_free[par])
                        exp_toks[kb] = SW.activation(
                            out=Ppt[:, par * 512 + q0:par * 512 + 512],
                            in_=pS[:, par * 512 + q0:par * 512 + 512],
                            func=AF.Exp, bias=bias_h[:, h:h + 1])
                        pS_free[par] = exp_toks[kb]
                        _wait(VE, exp_toks[kb])
                        sum_tok = VW.tensor_add(
                            sumacc[:, epar * 512 + q0:epar * 512 + 512],
                            sumacc[:, epar * 512 + q0:epar * 512 + 512],
                            Ppt[:, par * 512 + q0:par * 512 + 512])
                        PptSum_free[par] = sum_tok
                        if kb >= 2:
                            emit_av(kb - 2)
                    for kb in range(max(0, nkb - 2), nkb):
                        emit_av(kb)
                    av_done = _tok(S["pe"])
                    pend[0] = dict(i=i, qg=qg, qpar=qpar, epar=epar,
                                   sum_tok=sum_tok, av_done=av_done)
        emit_epi2()

    nc.all_engine_barrier()

    if cutoff == "p2":
        SY.wait_ge(S["o_st0"][0], 0)
        return nc
    wo_sb = sb("wo_sb", [128, 2 * QH * 128], BF16, RC)
    wo_f = sb("wo_f", [128, 2 * QH * 128], F32, RC + 4 * KB)
    stage = sb("stage", [128, 2 * 512], F32, RC + 12 * KB)
    PTS = min(512, TOWN)
    cc_toks = []
    with nc.psum_tensor("pm3", [128, 2 * 512], F32) as pm3:
        wo_free = [None, None]
        pm_free = [None, None]
        stage_free = [None, None]
        wof_free = [None, None]
        ftoks = [None] * NDC

        def wo_load(mm_):
            par_ = mm_ % 2
            _wait(SE, wof_free[par_])
            ld = SE.dma_start(
                out=wo_f[:, par_ * QH * 128:(par_ + 1) * QH * 128]
                .rearrange("p (q mm) -> p q mm", q=QH),
                in_=wo[mm_])
            ftoks[mm_] = _inc(S["wo0" if par_ == 0 else "wo1"], ld, 16)

        wo_load(0)
        if NDC > 1:
            wo_load(1)
        for m in range(NDC):
            par = m % 2
            _wait(SE, ftoks[m], wo_free[par])
            wtok = SW.activation(out=wo_sb[:, par * QH * 128:(par + 1) * QH * 128],
                                 in_=wo_f[:, par * QH * 128:(par + 1) * QH * 128],
                                 func=AF.Copy)
            wof_free[par] = wtok
            if m + 2 < NDC:
                wo_load(m + 2)
            for tgT in range(T // PTS):
                ppar = (m * (T // PTS) + tgT) % 2
                _wait(TE, wtok, pm_free[ppar])
                for qc in range(QH):
                    ins = TE.matmul(
                        pm3[:, ppar * 512:ppar * 512 + PTS],
                        wo_sb[:, (par * QH + qc) * 128:(par * QH + qc) * 128 + 128],
                        y_sb[:, qc * T + tgT * PTS:qc * T + (tgT + 1) * PTS],
                        start=(qc == 0), stop=(qc == QH - 1), skip_group_check=True)
                mmtok = _inc(S["pe"], ins)
                if tgT == T // PTS - 1:
                    wo_free[par] = mmtok
                _wait(VE, mmtok, stage_free[ppar])
                pm_free[ppar] = VW.tensor_copy(stage[:, ppar * 512:ppar * 512 + PTS],
                                               pm3[:, ppar * 512:ppar * 512 + PTS])
                _wait(SY, pm_free[ppar])
                r = (tgT * PTS) // TOWN
                cofs = (tgT * PTS) % TOWN
                band = m // (NDC // NBAND)
                mb = m % (NDC // NBAND)
                st = SY.dma_start(
                    out=part_drams[band][r * DB + mb * 128:r * DB + (mb + 1) * 128,
                                         cofs:cofs + PTS],
                    in_=stage[:, ppar * 512:ppar * 512 + PTS])
                stage_free[ppar] = _inc(S["p_st0" if ppar == 0 else "p_st1"], st, 16)
            if (m + 1) % (NDC // NBAND) == 0:
                band = m // (NDC // NBAND)
                GE.wait_ge(S["p_st0"][0], S["p_st0"][1])
                GE.wait_ge(S["p_st1"][0], S["p_st1"][1])
                cc = GE.collective_compute(
                    "ReduceScatter", OP.add,
                    replica_groups=[[0, 1, 2, 3], [4, 5, 6, 7]],
                    ins=[part_drams[band].opt()],
                    outs=[ao_drams[band].opt()],
                )
                cc_toks.append(_inc(S["cc"], cc))

    if cutoff == "p3a":
        SY.wait_ge(S["p_st0"][0], S["p_st0"][1])
        SY.wait_ge(S["p_st1"][0], S["p_st1"][1])
        return nc
    nc.all_engine_barrier()

    if cutoff == "rs":
        SY.wait_ge(S["o_st0"][0], 0)
        return nc
    vx1 = sb("vx1", [128, NDC * TOWN], BF16, RA)
    aob = sb("aob", [128, 2 * TOWN], F32, RC)
    xrb = sb("xrb", [128, 2 * TOWN], F32, RC + 8 * KB)
    mb_sb = sb("mb_sb", [128, 2 * 512], F32, RD)
    rb2_sb = sb("rb2_sb", [128, 2 * 512], F32, RD + 4 * KB)
    vsq = sb("vsq", [128, 512], F32, RD + 8 * KB)
    stat = sb("stat", [1, 8 * 512], F32, RE)
    vtmpf = sb("vtmpf", [128, 512], F32, RF)

    def ln_phase(read_v, stats_ps, bc_ps, write_out, out_stage):
        stats_done = _tok(S["pe"])
        for tg in range(NTG):
            _wait(VE, stats_done)
            VW.tensor_scalar_mul(stat[:, tg * 512:tg * 512 + TGS],
                                 stats_ps[:1, tg * 512:tg * 512 + TGS], 1.0 / D)
            VW.tensor_scalar_mul(stat[:, (2 + tg) * 512:(2 + tg) * 512 + TGS],
                                 stats_ps[:1, (2 + tg) * 512:(2 + tg) * 512 + TGS],
                                 1.0 / D)
            VW.tensor_mul(stat[:, (4 + tg) * 512:(4 + tg) * 512 + TGS],
                          stat[:, tg * 512:tg * 512 + TGS],
                          stat[:, tg * 512:tg * 512 + TGS])
            vtok = VW.tensor_sub(stat[:, (4 + tg) * 512:(4 + tg) * 512 + TGS],
                                 stat[:, (2 + tg) * 512:(2 + tg) * 512 + TGS],
                                 stat[:, (4 + tg) * 512:(4 + tg) * 512 + TGS])
            _wait(SE, vtok)
            SW.activation(out=stat[:, (6 + tg) * 512:(6 + tg) * 512 + TGS],
                          in_=stat[:, (4 + tg) * 512:(4 + tg) * 512 + TGS],
                          func=AF.Ln, bias=float(c.eps))
            atok = SW.activation(out=stat[:, (4 + tg) * 512:(4 + tg) * 512 + TGS],
                                 in_=stat[:, (6 + tg) * 512:(6 + tg) * 512 + TGS],
                                 func=AF.Exp, scale=-0.5)
            _wait(TE, atok)
            ins = TE.matmul(bc_ps[:, 0:TGS], ones1f[:],
                            stat[:, tg * 512:tg * 512 + TGS],
                            start=True, stop=True, skip_group_check=True)
            ttok = _inc(S["pe"], ins)
            _wait(VE, ttok)
            cptok = VW.tensor_copy(mb_sb[:, tg * 512:tg * 512 + TGS], bc_ps[:, 0:TGS])
            _wait(TE, cptok)
            ins = TE.matmul(bc_ps[:, 0:TGS], ones1f[:],
                            stat[:, (4 + tg) * 512:(4 + tg) * 512 + TGS],
                            start=True, stop=True, skip_group_check=True)
            ttok = _inc(S["pe"], ins)
            _wait(VE, ttok)
            VW.tensor_copy(rb2_sb[:, tg * 512:tg * 512 + TGS], bc_ps[:, 0:TGS])
        bc_done = _tok(S["dve"])
        return bc_done

    with (
        nc.psum_tensor("pst3", [128, 4 * 512], F32) as pst3,
        nc.psum_tensor("pbc3", [128, 512], F32) as pbc3,
    ):
        ao_free = [None, None]
        xr_free = [None, None]
        vtmpf_free = []
        for m in range(NDC):
            par = m % 2
            band = m // (NDC // NBAND)
            mb = m % (NDC // NBAND)
            _wait(SY, ao_free[par], cc_toks[band])
            st = SY.dma_start(out=aob[:, par * TOWN:(par + 1) * TOWN],
                              in_=ao_drams[band][mb * 128:(mb + 1) * 128, :])
            aotok = _inc(S["ao0" if par == 0 else "ao1"], st, 16)
            _wait(SY, xr_free[par])
            st = SY.dma_start(out=xrb[:, par * TOWN:(par + 1) * TOWN],
                              in_=xro[:, m, :])
            xrtok = _inc(S["xr0" if par == 0 else "xr1"], st, 16)
            for tg in range(NTG):
                _wait(VE, aotok, xrtok, *vtmpf_free)
                vtok = VW.tensor_add(vtmpf[:, :TGS],
                                     aob[:, par * TOWN + tg * TGS:
                                         par * TOWN + (tg + 1) * TGS],
                                     xrb[:, par * TOWN + tg * TGS:
                                         par * TOWN + (tg + 1) * TGS])
                _wait(SE, vtok, *vtmpf_free)
                sqtok = SW.activation(out=vsq[:, :TGS], in_=vtmpf[:, :TGS], func=AF.Square)
                _wait(TE, vtok)
                ins = TE.matmul(pst3[:1, tg * 512:tg * 512 + TGS], ones128f[:],
                                vtmpf[:, :TGS], start=(m == 0), stop=(m == NDC - 1),
                                skip_group_check=True)
                _inc(S["pe"], ins)
                _wait(TE, sqtok)
                ins = TE.matmul(pst3[:1, (2 + tg) * 512:(2 + tg) * 512 + TGS],
                                ones128f[:], vsq[:, :TGS],
                                start=(m == 0), stop=(m == NDC - 1),
                                skip_group_check=True)
                sttok = _inc(S["pe"], ins)
                vctok = VW.tensor_copy(
                    vx1[:, m * TOWN + tg * TGS:m * TOWN + (tg + 1) * TGS],
                    vtmpf[:, :TGS])
                vtmpf_free = [sqtok, sttok]
                if tg == NTG - 1:
                    ao_free[par] = vctok
                    xr_free[par] = vctok
        ln_phase(None, pst3, pbc3, None, None)
        for m in range(NDC):
            for tg in range(NTG):
                sl = slice(m * TOWN + tg * TGS, m * TOWN + (tg + 1) * TGS)
                VW.tensor_sub(vtmpf[:, :TGS], vx1[:, sl],
                              mb_sb[:, tg * 512:tg * 512 + TGS])
                VW.tensor_mul(vx1[:, sl], vtmpf[:, :TGS],
                              rb2_sb[:, tg * 512:tg * 512 + TGS])
        _wait(SY, _tok(S["dve"]))
        st = SY.dma_start(out=x1_dram,
                          in_=vx1[:].rearrange("p (d t) -> p d t", d=NDC))
        _inc(S["x1_st"], st, 16)

    nc.all_engine_barrier()

    if cutoff == "p3b":
        SY.wait_ge(S["o_st0"][0], 0)
        return nc
    acc1 = sb("acc1", [128, c.MH * TOWN], F32, RB)
    g_grp = sb("g_grp", [128, 8 * TOWN], BF16, RD)
    w13_sb = sb("w13_sb", [128, 2 * 2 * NDC * 128], BF16, RC)
    w2s_sb = sb("w2s_sb", [128, 2 * 8 * 4 * 128], BF16, RE)
    at_sb = sb("at_sb", [128, 2 * NDC * 16], BF16, RF)
    a2_sb = sb("a2_sb", [128, NJ * 16], BF16, RF + 2 * NDC * 16 * 2)
    xa1s = sb("xa1s", [16, TOWN], BF16, RF + 5 * KB)
    xa3s = sb("xa3s", [16, TOWN], BF16, RF + 5 * KB + 2 * TOWN)
    xa2s = sb("xa2s", [16, TOWN], BF16, RF + 5 * KB + 4 * TOWN)
    b1_sb = sb("b1_sb", [16, 2 * 128], BF16, RG + 2 * KB)
    b3_sb = sb("b3_sb", [16, 2 * 128], BF16, RG + 2 * KB + 512)
    sg1 = sb("sg1", [128, 512], BF16, RG + 3 * KB)

    _inc(S["misc"], GE.dma_start(
        out=at_sb[:, 0:NDC * 16].rearrange("p (d r) -> p d r", d=NDC), in_=a1t), 16)
    _inc(S["misc"], GE.dma_start(
        out=at_sb[:, NDC * 16:2 * NDC * 16].rearrange("p (d r) -> p d r", d=NDC),
        in_=a3t), 16)
    _inc(S["misc"], GE.dma_start(
        out=a2_sb[:].rearrange("p (j r) -> p j r", j=NJ), in_=a2t), 16)
    at_tok = _tok(S["misc"])

    with nc.psum_tensor("pxa", [16, 512], F32) as pxa:
        _wait(TE, at_tok)
        for wi, xout in [(0, xa1s), (1, xa3s)]:
            for tg in range(NTG):
                for dc in range(NDC):
                    ins = TE.matmul(
                        pxa[:, :TGS],
                        at_sb[:, wi * NDC * 16 + dc * 16:wi * NDC * 16 + (dc + 1) * 16],
                        vx1[:, dc * TOWN + tg * TGS:dc * TOWN + (tg + 1) * TGS],
                        start=(dc == 0), stop=(dc == NDC - 1), skip_group_check=True)
                xtok = _inc(S["pe"], ins)
                _wait(SE, xtok)
                atok = SW.activation(out=xout[:, tg * TGS:(tg + 1) * TGS],
                                     in_=pxa[:, :TGS],
                                     func=AF.Copy, scale=float(c.lora_scale))
                _wait(TE, atok)

    with (
        nc.psum_tensor("ph", [128, 4 * 512], F32) as ph,
        nc.psum_tensor("pw2", [128, 2 * 512], F32) as pw2,
        nc.psum_tensor("pxa2", [16, NTG * 512], F32) as pxa2,
    ):
        h_free = [None, None]
        w13_free = [None, None]
        b13_free = [None, None]
        sg1_free = None
        g_pe_free = None
        g_st_free = [None, None]
        pw2_free = [None, None]
        first_acc = [True] * (c.MH * NTG)
        w2s_free = [None, None]
        unit = 0
        nj_done = 0
        for gi, js in enumerate(c.G1):
            nj_g = len(js)
            for jj, j in enumerate(js):
                par = (nj_done + jj) % 2
                _wait(GE, w13_free[par])
                ld = GE.dma_start(
                    out=w13_sb[:, par * 2 * NDC * 128:par * 2 * NDC * 128 + NDC * 128]
                    .rearrange("p (d mm) -> p d mm", d=NDC), in_=w1b[j])
                _inc(S["w1a0" if par == 0 else "w1a1"], ld, 16)
                _wait(GE, b13_free[par])
                ld = GE.dma_start(out=b1_sb[:, par * 128:(par + 1) * 128], in_=b1t[j])
                wtok1 = _inc(S["w1a0" if par == 0 else "w1a1"], ld, 16)
                ld = GE.dma_start(
                    out=w13_sb[:, par * 2 * NDC * 128 + NDC * 128:
                               (par + 1) * 2 * NDC * 128]
                    .rearrange("p (d mm) -> p d mm", d=NDC), in_=w3b[j])
                _inc(S["w3a0" if par == 0 else "w3a1"], ld, 16)
                ld = GE.dma_start(out=b3_sb[:, par * 128:(par + 1) * 128], in_=b3t[j])
                wtok = _inc(S["w3a0" if par == 0 else "w3a1"], ld, 16)
                for tg in range(NTG):
                    upar = unit % 2
                    unit += 1
                    _wait(TE, wtok1, h_free[upar])
                    base = par * 2 * NDC * 128
                    for dc in range(NDC):
                        TE.matmul(
                            ph[:, upar * 1024:upar * 1024 + TGS],
                            w13_sb[:, base + dc * 128:base + dc * 128 + 128],
                            vx1[:, dc * TOWN + tg * TGS:dc * TOWN + (tg + 1) * TGS],
                            start=(dc == 0), stop=False, skip_group_check=True)
                    TE.matmul(ph[:, upar * 1024:upar * 1024 + TGS],
                              b1_sb[:, par * 128:(par + 1) * 128],
                              xa1s[:, tg * TGS:(tg + 1) * TGS],
                              start=False, stop=True, skip_group_check=True)
                    _wait(TE, wtok)
                    for dc in range(NDC):
                        TE.matmul(
                            ph[:, upar * 1024 + 512:upar * 1024 + 512 + TGS],
                            w13_sb[:, base + NDC * 128 + dc * 128:
                                   base + NDC * 128 + dc * 128 + 128],
                            vx1[:, dc * TOWN + tg * TGS:dc * TOWN + (tg + 1) * TGS],
                            start=(dc == 0), stop=False, skip_group_check=True)
                    ins = TE.matmul(ph[:, upar * 1024 + 512:upar * 1024 + 512 + TGS],
                                    b3_sb[:, par * 128:(par + 1) * 128],
                                    xa3s[:, tg * TGS:(tg + 1) * TGS],
                                    start=False, stop=True, skip_group_check=True)
                    htok = _inc(S["pe"], ins)
                    if tg == NTG - 1:
                        w13_free[par] = htok
                        b13_free[par] = htok
                    _wait(SE, htok, sg1_free)
                    atok = SW.activation(out=sg1[:, :TGS],
                                         in_=ph[:, upar * 1024:upar * 1024 + TGS],
                                         func=AF.Silu)
                    gsl = slice(jj * TOWN + tg * TGS, jj * TOWN + (tg + 1) * TGS)
                    _wait(VE, atok, g_pe_free, *g_st_free)
                    gtok = VW.tensor_mul(
                        g_grp[:, gsl], sg1[:, :TGS],
                        ph[:, upar * 1024 + 512:upar * 1024 + 512 + TGS])
                    h_free[upar] = gtok
                    sg1_free = gtok
                    _wait(TE, gtok)
                    ins = TE.matmul(pxa2[:, tg * 512:tg * 512 + TGS],
                                    a2_sb[:, j * 16:(j + 1) * 16],
                                    g_grp[:, gsl],
                                    start=(nj_done + jj == 0), stop=(j == NJ - 1),
                                    skip_group_check=True)
                    _inc(S["pe"], ins)
                _wait(SY, _tok(S["dve"]))
                st = SY.dma_start(out=g_dram[j],
                                  in_=g_grp[:, jj * TOWN:(jj + 1) * TOWN])
                _inc(S["g_st0" if jj % 2 == 0 else "g_st1"], st, 16)
            for mq in range(0, c.MH, 4):
                wpar = (mq // 4) % 2
                nm = min(4, c.MH - mq)
                _wait(GE, w2s_free[wpar])
                ld = GE.dma_start(
                    out=w2s_sb[:, wpar * 8 * 4 * 128:
                               wpar * 8 * 4 * 128 + nj_g * nm * 128]
                    .rearrange("p (j mx) -> p j mx", j=nj_g),
                    in_=w2b[js[0]:js[0] + nj_g, :, mq:mq + nm, :]
                    .rearrange("j p mm x -> p j (mm x)"))
                w2tok = _inc(S["w2ld0" if wpar == 0 else "w2ld1"], ld, 16)
                for mi in range(nm):
                    mcur = mq + mi
                    for tg in range(NTG):
                        ppar = (mcur * NTG + tg) % 2
                        _wait(TE, w2tok, pw2_free[ppar])
                        for jj in range(nj_g):
                            ins = TE.matmul(
                                pw2[:, ppar * 512:ppar * 512 + TGS],
                                w2s_sb[:, wpar * 8 * 4 * 128 + (jj * nm + mi) * 128:
                                       wpar * 8 * 4 * 128 + (jj * nm + mi) * 128 + 128],
                                g_grp[:, jj * TOWN + tg * TGS:
                                      jj * TOWN + (tg + 1) * TGS],
                                start=(jj == 0), stop=(jj == nj_g - 1),
                                skip_group_check=True)
                        mmtok = _inc(S["pe"], ins)
                        _wait(VE, mmtok)
                        asl = slice(mcur * TOWN + tg * TGS, mcur * TOWN + (tg + 1) * TGS)
                        if first_acc[mcur * NTG + tg]:
                            tk = VW.tensor_copy(acc1[:, asl],
                                                pw2[:, ppar * 512:ppar * 512 + TGS])
                            first_acc[mcur * NTG + tg] = False
                        else:
                            tk = VW.tensor_add(acc1[:, asl], acc1[:, asl],
                                               pw2[:, ppar * 512:ppar * 512 + TGS])
                        pw2_free[ppar] = tk
                w2s_free[wpar] = _tok(S["pe"])
            g_pe_free = _tok(S["pe"])
            g_st_free = [_tok(S["g_st0"]), _tok(S["g_st1"])]
            nj_done += nj_g
        xa2_done = _tok(S["pe"])
        for tg in range(NTG):
            _wait(SE, xa2_done)
            SW.activation(out=xa2s[:, tg * TGS:(tg + 1) * TGS],
                          in_=pxa2[:, tg * 512:tg * 512 + TGS],
                          func=AF.Copy, scale=float(c.lora_scale))

    nc.all_engine_barrier()

    if cutoff == "p4a":
        SY.wait_ge(S["o_st0"][0], 0)
        return nc
    acc2 = sb("acc2", [128, (NDC - c.MH) * TOWN], F32, RA)
    g2_sb = sb("g2_sb", [128, 2 * 4 * TOWN], BF16, RD)
    w2s2 = sb("w2s2", [128, 2 * 4 * 4 * 128], BF16, RE)
    b2_sb = sb("b2_sb", [16, NDC * 128], BF16, RC)

    with (
        nc.psum_tensor("pw2b", [128, 2 * 512], F32) as pw2,
        nc.psum_tensor("plora", [128, 2 * 512], F32) as plora,
    ):
        _inc(S["misc"], GE.dma_start(
            out=b2_sb[:].rearrange("p (d mm) -> p d mm", d=NDC), in_=b2t), 16)
        b2_tok = _tok(S["misc"])
        SY.wait_ge(S["g_st0"][0], S["g_st0"][1])
        SY.wait_ge(S["g_st1"][0], S["g_st1"][1])
        VE.wait_ge(S["x1_st"][0], S["x1_st"][1])
        g_free2 = [None, None]
        pw2_free = [None, None]
        first2 = [True] * ((NDC - c.MH) * NTG)
        sched = []
        for gi, js in enumerate(c.G2):
            for mq in range(c.MH, NDC, 4):
                sched.append((gi, js, gi % 2, mq, min(4, NDC - mq)))
        w2chunk_free = [None, None]
        w2toks = [None] * len(sched)
        gtoks = [None] * len(c.G2)

        def emit_w2_dma(ci):
            gi, js, gpar, mq, nm = sched[ci]
            nj_g = len(js)
            wp = ci % 2
            if gtoks[gi] is None:
                _wait(SY, g_free2[gpar])
                ld = SY.dma_start(
                    out=g2_sb[:, gpar * 4 * TOWN:gpar * 4 * TOWN + nj_g * TOWN]
                    .rearrange("p (j t) -> p j t", j=nj_g),
                    in_=g_dram[js[0]:js[0] + nj_g].rearrange("j p t -> p j t"))
                gtoks[gi] = _inc(S["gl0" if gpar == 0 else "gl1"], ld, 16)
            _wait(GE, w2chunk_free[wp])
            ld = GE.dma_start(
                out=w2s2[:, wp * 4 * 4 * 128:wp * 4 * 4 * 128 + nj_g * nm * 128]
                .rearrange("p (j mx) -> p j mx", j=nj_g),
                in_=w2b[js[0]:js[0] + nj_g, :, mq:mq + nm, :]
                .rearrange("j p mm x -> p j (mm x)"))
            w2toks[ci] = _inc(S["w2ld0" if wp == 0 else "w2ld1"], ld, 16)

        emit_w2_dma(0)
        if len(sched) > 1:
            emit_w2_dma(1)
        for ci, (gi, js, gpar, mq, nm) in enumerate(sched):
            nj_g = len(js)
            wp = ci % 2
            for mi in range(nm):
                mcur = mq + mi
                m2 = mcur - c.MH
                for tg in range(NTG):
                    ppar = (m2 * NTG + tg) % 2
                    _wait(TE, w2toks[ci], gtoks[gi], pw2_free[ppar])
                    for jj in range(nj_g):
                        ins = TE.matmul(
                            pw2[:, ppar * 512:ppar * 512 + TGS],
                            w2s2[:, wp * 4 * 4 * 128 + (jj * nm + mi) * 128:
                                 wp * 4 * 4 * 128 + (jj * nm + mi) * 128 + 128],
                            g2_sb[:, gpar * 4 * TOWN + jj * TOWN + tg * TGS:
                                  gpar * 4 * TOWN + jj * TOWN + (tg + 1) * TGS],
                            start=(jj == 0), stop=(jj == nj_g - 1),
                            skip_group_check=True)
                    mmtok = _inc(S["pe"], ins)
                    _wait(VE, mmtok)
                    asl = slice(m2 * TOWN + tg * TGS, m2 * TOWN + (tg + 1) * TGS)
                    if first2[m2 * NTG + tg]:
                        tk = VW.tensor_copy(acc2[:, asl],
                                            pw2[:, ppar * 512:ppar * 512 + TGS])
                        first2[m2 * NTG + tg] = False
                    else:
                        tk = VW.tensor_add(acc2[:, asl], acc2[:, asl],
                                           pw2[:, ppar * 512:ppar * 512 + TGS])
                    pw2_free[ppar] = tk
            w2chunk_free[wp] = _tok(S["pe"])
            if ci == len(sched) - 1 or sched[ci + 1][0] != gi:
                g_free2[gpar] = _tok(S["pe"])
            if ci + 2 < len(sched):
                emit_w2_dma(ci + 2)
        plora_free = [None, None]
        for m in range(NDC):
            tgt = acc1 if m < c.MH else acc2
            mm2 = m if m < c.MH else m - c.MH
            for tg in range(NTG):
                ppar = (m * NTG + tg) % 2
                _wait(TE, b2_tok, plora_free[ppar])
                ins = TE.matmul(plora[:, ppar * 512:ppar * 512 + TGS],
                                b2_sb[:, m * 128:(m + 1) * 128],
                                xa2s[:, tg * TGS:(tg + 1) * TGS],
                                start=True, stop=True, skip_group_check=True)
                mtok = _inc(S["pe"], ins)
                _wait(VE, mtok)
                asl = slice(mm2 * TOWN + tg * TGS, mm2 * TOWN + (tg + 1) * TGS)
                plora_free[ppar] = VW.tensor_add(
                    tgt[:, asl], tgt[:, asl],
                    plora[:, ppar * 512:ppar * 512 + TGS])

    nc.all_engine_barrier()

    if cutoff == "p4b":
        SY.wait_ge(S["o_st0"][0], 0)
        return nc
    x1c = sb("x1c", [128, 2 * 512], BF16, RC)
    outb = sb("outb", [128, 2 * 512], F32, RC + 2 * KB)
    with (
        nc.psum_tensor("pst5", [128, 4 * 512], F32) as pst5,
        nc.psum_tensor("pb5", [128, 512], F32) as pb5,
    ):
        x1_free = [None, None]
        vsq_free = None
        SY.wait_ge(S["x1_st"][0], S["x1_st"][1])
        for m in range(NDC):
            tgt = acc1 if m < c.MH else acc2
            mm2 = m if m < c.MH else m - c.MH
            for tg in range(NTG):
                par = (m * NTG + tg) % 2
                _wait(SY, x1_free[par])
                st = SY.dma_start(out=x1c[:, par * 512:par * 512 + TGS],
                                  in_=x1_dram[:, m, tg * TGS:(tg + 1) * TGS])
                xtok = _inc(S["x1_ld0" if par == 0 else "x1_ld1"], st, 16)
                asl = slice(mm2 * TOWN + tg * TGS, mm2 * TOWN + (tg + 1) * TGS)
                _wait(VE, xtok)
                vtok = VW.tensor_add(tgt[:, asl], tgt[:, asl],
                                     x1c[:, par * 512:par * 512 + TGS])
                x1_free[par] = vtok
                _wait(SE, vtok, vsq_free)
                sqtok = SW.activation(out=vsq[:, :TGS], in_=tgt[:, asl],
                                      func=AF.Square)
                _wait(TE, vtok)
                ins = TE.matmul(pst5[:1, tg * 512:tg * 512 + TGS], ones128f[:],
                                tgt[:, asl], start=(m == 0), stop=(m == NDC - 1),
                                skip_group_check=True)
                _inc(S["pe"], ins)
                _wait(TE, sqtok)
                ins = TE.matmul(pst5[:1, (2 + tg) * 512:(2 + tg) * 512 + TGS],
                                ones128f[:], vsq[:, :TGS],
                                start=(m == 0), stop=(m == NDC - 1),
                                skip_group_check=True)
                vsq_free = _inc(S["pe"], ins)
        bc_done = ln_phase(None, pst5, pb5, None, None)
        ob_free = [None, None]
        for m in range(NDC):
            tgt = acc1 if m < c.MH else acc2
            mm2 = m if m < c.MH else m - c.MH
            for tg in range(NTG):
                par = (m * NTG + tg) % 2
                asl = slice(mm2 * TOWN + tg * TGS, mm2 * TOWN + (tg + 1) * TGS)
                _wait(VE, bc_done, ob_free[par])
                VW.tensor_sub(vtmpf[:, :TGS], tgt[:, asl],
                              mb_sb[:, tg * 512:tg * 512 + TGS])
                otok = VW.tensor_mul(outb[:, par * 512:par * 512 + TGS],
                                     vtmpf[:, :TGS],
                                     rb2_sb[:, tg * 512:tg * 512 + TGS])
                _wait(SY, otok)
                st = SY.dma_start(out=out_fm[:, m, tg * TGS:(tg + 1) * TGS],
                                  in_=outb[:, par * 512:par * 512 + TGS])
                ob_free[par] = _inc(S["o_st0" if par == 0 else "o_st1"], st, 16)
        SY.wait_ge(S["o_st0"][0], S["o_st0"][1])
        SY.wait_ge(S["o_st1"][0], S["o_st1"][1])

    return nc



def _c(x):
    return np.ascontiguousarray(x)


def make_core_inputs(c: Cfg, x, w_qkv, w_o, w1, w3, w2, A1, B1, A3, B3, A2, B2):
    D, T, NDC, NJ = c.D, c.T, c.NDC, c.NJ
    QS = c.NH * c.HD
    KVS = c.NKV * c.HD

    invf = 1.0 / c.theta ** (np.arange(0, c.HD, 2, dtype=np.float32) / c.HD)
    pos = np.arange(T, dtype=np.float32)
    fr = np.einsum("t,f->tf", pos, invf)
    emb = np.concatenate([fr, fr], axis=-1)
    cs = _c(np.stack([np.cos(emb).T, np.sin(emb).T], axis=1)
            .astype(ml_dtypes.bfloat16))

    mask = np.where(np.arange(128)[:, None] <= np.arange(128)[None, :],
                    0.0, -1e30).astype(np.float32)
    R = np.zeros((128, 128), np.float32)
    for ii in range(64):
        R[ii, ii + 64] = -1.0
        R[ii + 64, ii] = 1.0
    consts = np.zeros((128, 520), np.float32)
    consts[:, 0:128] = mask
    consts[:, 128:256] = R.T
    consts[:, 256:384] = np.eye(128, dtype=np.float32)
    consts[:, 384] = 1.0
    consts[:, 385] = 1.0
    consts[0, 386:514] = 1.0
    consts[:, 514] = np.float32(c.eps)
    consts[:, 515] = 0.5
    consts[:, 516] = -0.5
    consts[:, 517] = np.float32(c.scaling)
    consts[:, 518] = np.float32(c.lora_scale)
    consts[:, 519] = 1.0
    consts = _c(consts)

    w1bb = _c(w1.reshape(NJ, 128, NDC, 128).transpose(0, 3, 2, 1))
    w3bb = _c(w3.reshape(NJ, 128, NDC, 128).transpose(0, 3, 2, 1))
    w2bb = _c(w2.reshape(NDC, 128, NJ, 128).transpose(2, 3, 0, 1))
    a1tb = _c(A1.reshape(16, NDC, 128).transpose(2, 1, 0))
    a3tb = _c(A3.reshape(16, NDC, 128).transpose(2, 1, 0))
    a2tb = _c(A2.reshape(16, NJ, 128).transpose(2, 1, 0))
    b1tb = _c(B1.reshape(NJ, 128, 16).transpose(0, 2, 1))
    b3tb = _c(B3.reshape(NJ, 128, 16).transpose(0, 2, 1))
    b2tb = _c(B2.reshape(NDC, 128, 16).transpose(2, 0, 1))

    in_maps = []
    for core in range(8):
        b = core // 4
        g4 = core % 4
        xb = np.asarray(x[b])
        xwc = _c(xb.reshape(c.XGN, 512, NDC, 128).transpose(3, 0, 2, 1))
        xown = xb[g4 * c.TOWN:(g4 + 1) * c.TOWN]
        xroc = _c(xown.reshape(c.TOWN, NDC, 128).transpose(2, 1, 0))
        cols = []
        for hh in range(c.KVH):
            h0 = 2 * g4 + hh
            cols.append(w_qkv[:, QS + h0 * 128:QS + h0 * 128 + 128])
        for hh in range(c.KVH):
            h0 = 2 * g4 + hh
            cols.append(w_qkv[:, QS + KVS + h0 * 128:QS + KVS + h0 * 128 + 128])
        wkv_c = np.stack(cols, axis=1)
        wkv_c = _c(wkv_c.reshape(NDC, 128, 4, 128).transpose(1, 2, 0, 3))
        wq_c = w_qkv[:, c.QH * g4 * 128:(c.QH * g4 + c.QH) * 128]
        wq_c = _c(wq_c.reshape(NDC, 128, c.QH, 128).transpose(1, 2, 0, 3))
        wo_c = w_o[c.QH * g4 * 128:(c.QH * g4 + c.QH) * 128, :]
        wo_c = _c(wo_c.reshape(c.QH, 128, NDC, 128).transpose(2, 1, 0, 3))
        in_maps.append(dict(
            xw=xwc, xro=xroc, wkv=wkv_c, wq=wq_c, wo=wo_c, w1b=w1bb, w3b=w3bb,
            w2b=w2bb, a1t=a1tb, a3t=a3tb, a2t=a2tb, b1t=b1tb, b3t=b3tb, b2t=b2tb,
            cs=cs, consts=consts,
        ))
    return in_maps


def assemble_output(c: Cfg, results):
    out = np.empty((2, c.T, c.D), np.float32)
    for core in range(8):
        b = core // 4
        g4 = core % 4
        o = results[core]["out_fm"]
        out[b, g4 * c.TOWN:(g4 + 1) * c.TOWN, :] = (
            o.transpose(2, 1, 0).reshape(c.TOWN, c.D))
    return out


def kernel(L0_attn_hidden_states, L0_attn_w_qkv, L0_attn_w_o, L0_ffn_w1, L0_ffn_w3,
           L0_ffn_w2, L0_ffn_lora_A1, L0_ffn_lora_B1, L0_ffn_lora_A3, L0_ffn_lora_B3,
           L0_ffn_lora_A2, L0_ffn_lora_B2):
    c = Cfg()
    in_maps = make_core_inputs(
        c, np.asarray(L0_attn_hidden_states), np.asarray(L0_attn_w_qkv),
        np.asarray(L0_attn_w_o), np.asarray(L0_ffn_w1), np.asarray(L0_ffn_w3),
        np.asarray(L0_ffn_w2), np.asarray(L0_ffn_lora_A1), np.asarray(L0_ffn_lora_B1),
        np.asarray(L0_ffn_lora_A3), np.asarray(L0_ffn_lora_B3),
        np.asarray(L0_ffn_lora_A2), np.asarray(L0_ffn_lora_B2))
    nc = build_nc(c)
    res = run_bass_kernel_spmd(nc, in_maps, core_ids=list(range(8)))
    return assemble_output(c, res.results)


# version 09
# speedup vs baseline: 1.0664x; 1.0664x over previous
import numpy as np
import ml_dtypes
import concourse.bass as bass
import concourse.mybir as mybir
from concourse.bass_utils import run_bass_kernel_spmd

F32 = mybir.dt.float32
BF16 = mybir.dt.bfloat16
AF = mybir.ActivationFunctionType
OP = mybir.AluOpType
KB = 1024


class Cfg:
    def __init__(self, D=4096, T=4096, HID=11008, NH=32, NKV=8):
        self.D, self.T, self.HID, self.NH, self.NKV = D, T, HID, NH, NKV
        self.HD = 128
        self.theta = 10000.0
        self.eps = 1e-5
        self.lora_scale = 2.0
        self.NDC = D // 128
        self.NJ = HID // 128
        self.NKB = T // 128
        self.XGN = T // 512
        self.QGN = T // 512
        self.QH = NH // 4
        self.KVH = NKV // 4
        self.TOWN = T // 4
        self.TGS = min(512, self.TOWN)
        self.NTG = self.TOWN // self.TGS
        self.MH = self.NDC // 2
        self.G1 = [list(range(s, min(s + 8, self.NJ))) for s in range(0, self.NJ, 8)]
        self.G2 = [list(range(s, min(s + 4, self.NJ))) for s in range(0, self.NJ, 4)]
        self.scaling = self.HD ** (-0.5)


def _sem(nc, name):
    return [nc.alloc_semaphore(name), 0]


def _inc(sem, ins, k=1):
    ins.then_inc(sem[0], k)
    sem[1] += k
    return (sem[0], sem[1])


def _tok(sem):
    return (sem[0], sem[1])


_WAITED = {}


def _wait(eng, *toks):
    seen = _WAITED.setdefault(id(eng), {})
    for tok in toks:
        if tok is None:
            continue
        key = tok[0].num if hasattr(tok[0], "num") else id(tok[0])
        if seen.get(key, -1) >= tok[1]:
            continue
        seen[key] = tok[1]
        eng.wait_ge(tok[0], tok[1])


class EW:

    def __init__(self, eng, sem):
        self._e = eng
        self._s = sem

    def __getattr__(self, name):
        f = getattr(self._e, name)

        def g(*a, **k):
            self._e.wait_ge(self._s[0], self._s[1])
            return _inc(self._s, f(*a, **k))

        return g


def build_nc(c: Cfg, cutoff=None):
    _WAITED.clear()
    import os
    cutoff = cutoff or os.environ.get("KK_CUTOFF")
    nc = bass.Bass("TRN2", target_bir_lowering=False)
    T, D, NDC, NJ, QH = c.T, c.D, c.NDC, c.NJ, c.QH
    TOWN, TGS, NTG = c.TOWN, c.TGS, c.NTG
    NT512 = T // 512

    xw = nc.dram_tensor("xw", [128, c.XGN, NDC, 512], F32, kind="ExternalInput").ap()
    xro = nc.dram_tensor("xro", [128, NDC, TOWN], F32, kind="ExternalInput").ap()
    wkv = nc.dram_tensor("wkv", [128, 2 * c.KVH, NDC, 128], F32,
                         kind="ExternalInput").ap()
    wq = nc.dram_tensor("wq", [128, QH, NDC, 128], F32, kind="ExternalInput").ap()
    wo = nc.dram_tensor("wo", [NDC, 128, QH, 128], F32, kind="ExternalInput").ap()
    w1b = nc.dram_tensor("w1b", [NJ, 128, NDC, 128], F32, kind="ExternalInput").ap()
    w3b = nc.dram_tensor("w3b", [NJ, 128, NDC, 128], F32, kind="ExternalInput").ap()
    w2b = nc.dram_tensor("w2b", [NJ, 128, NDC, 128], F32, kind="ExternalInput").ap()
    a1t = nc.dram_tensor("a1t", [128, NDC, 16], F32, kind="ExternalInput").ap()
    a3t = nc.dram_tensor("a3t", [128, NDC, 16], F32, kind="ExternalInput").ap()
    a2t = nc.dram_tensor("a2t", [128, NJ, 16], F32, kind="ExternalInput").ap()
    b1t = nc.dram_tensor("b1t", [NJ, 16, 128], F32, kind="ExternalInput").ap()
    b3t = nc.dram_tensor("b3t", [NJ, 16, 128], F32, kind="ExternalInput").ap()
    b2t = nc.dram_tensor("b2t", [16, NDC, 128], F32, kind="ExternalInput").ap()
    cs = nc.dram_tensor("cs", [128, 2, T], BF16, kind="ExternalInput").ap()
    consts = nc.dram_tensor("consts", [128, 520], F32, kind="ExternalInput").ap()
    out_fm = nc.dram_tensor("out_fm", [128, NDC, TOWN], F32,
                            kind="ExternalOutput").ap()

    NBAND = 4 if NDC % 4 == 0 else 1
    DB = D // NBAND
    part_drams = [nc.dram_tensor(f"part_dram{b}", [4 * DB, TOWN], F32,
                                 kind="Internal").ap() for b in range(NBAND)]
    ao_drams = [nc.dram_tensor(f"ao_dram{b}", [DB, TOWN], F32,
                               kind="Internal").ap() for b in range(NBAND)]
    g_dram = nc.dram_tensor("g_dram", [NJ, 128, TOWN], BF16, kind="Internal").ap()
    x1_dram = nc.dram_tensor("x1_dram", [128, NDC, TOWN], BF16, kind="Internal").ap()
    x16_dram = nc.dram_tensor("x16_dram", [128, c.XGN, NDC, 512], BF16,
                              kind="Internal").ap()

    B0 = (nc.sbuf_base + 127) & ~127
    assert B0 + 207 * KB <= nc.sbuf_top, (B0, nc.sbuf_top)
    RA, RB, RC, RD, RE, RF, RG = (0, 64 * KB, 128 * KB, 160 * KB, 176 * KB,
                                  192 * KB, 203 * KB)

    def sb(name, shape, dtype, off):
        return nc.alloc_sbuf_tensor_at(name, shape, dtype, offset=B0 + off)

    mask_sb = sb("mask_sb", [128, 128], F32, RG)
    rt_sb = sb("rt_sb", [128, 128], BF16, RG + 512)
    id_sb = sb("id_sb", [128, 128], BF16, RG + 768)
    ones128 = sb("ones128", [128, 1], BF16, RG + 1024)
    ones128f = sb("ones128f", [128, 1], F32, RG + 1056)
    ones1f = sb("ones1f", [1, 128], F32, RG + 1088)
    csc_sb = sb("csc_sb", [128, 6], F32, RG + 1600)

    sems = {}
    for name in ["c_ld", "c_ldg", "cs0", "cs1", "x16a", "x16b", "xs0", "xs1", "wo0", "wo1", "wq0", "wq1", "wkv2", "w1a0", "w1a1", "w3a0", "w3a1", "xa0", "xa1", "wld0", "wld1", "pe", "act", "dve", "pool",
                 "st0", "st1", "cc", "g_st0", "g_st1", "x1_st",
                 "x1_ld0", "x1_ld1", "o_st0", "o_st1", "xr0", "xr1", "gl0", "gl1",
                 "w2ld0", "w2ld1", "ao0", "ao1", "misc", "p_st0", "p_st1", "p_st2", "p_st3"]:
        sems[name] = _sem(nc, name)
    S = sems

    TE, VE, SE, GE, SY = nc.tensor, nc.vector, nc.scalar, nc.gpsimd, nc.sync
    VW = EW(VE, S["dve"])
    SW = EW(SE, S["act"])
    GW = EW(GE, S["pool"])

    _inc(S["c_ld"], SY.dma_start(out=mask_sb[:], in_=consts[:, 0:128]), 16)
    _inc(S["c_ldg"], GE.dma_start(out=rt_sb[:], in_=consts[:, 128:256]), 16)
    _inc(S["c_ldg"], GE.dma_start(out=id_sb[:], in_=consts[:, 256:384]), 16)
    _inc(S["c_ldg"], GE.dma_start(out=ones128[:], in_=consts[:, 384:385],
                                  allow_slow_non_contiguous=True), 16)
    _inc(S["c_ld"], SY.dma_start(out=ones128f[:], in_=consts[:, 385:386],
                                 allow_slow_non_contiguous=True), 16)
    _inc(S["c_ld"], SY.dma_start(out=ones1f[:], in_=consts[0:1, 386:514]), 16)
    _inc(S["c_ld"], SY.dma_start(out=csc_sb[:], in_=consts[:, 514:520]), 16)
    for _i, _v in enumerate([c.eps, 0.5, -0.5, c.scaling, c.lora_scale, 1.0]):
        nc.const_aps.register(F32, float(_v), csc_sb[:, _i:_i + 1])

    K_sb = sb("K_sb", [128, c.KVH * T], BF16, RD)
    V_tm = sb("V_tm", [128, c.KVH * T], BF16, RE)
    Q_sb = sb("Q_sb", [128, QH * T], BF16, RA)
    y_sb = sb("y_sb", [128, QH * T], BF16, RB)
    c_ld_tok = (_tok(S["c_ld"]), _tok(S["c_ldg"]))

    t1_sb = sb("t1_sb", [128, 512], F32, RF)
    t2_sb = sb("t2_sb", [128, 512], F32, RF + 2 * KB)
    kt_sb = sb("kt_sb", [128, 2 * 512], BF16, RF + 4 * KB)
    vtmp = sb("vtmp", [128, 512], BF16, RF + 6 * KB)
    cs_t = sb("cs_t", [128, 2 * 2 * 512], BF16, RF + 7 * KB)

    HDC = NDC // 2
    hseq = [(xg, hf) for xg in range(c.XGN) for hf in range(2)]

    def proj_phase(out_buf, w_sb, w_tok, psum, nprj, rope, first_units, dest_base,
                   scale, xa_sb, rst, from_x16=False, write_x16=False):
        xa_free = rst.setdefault("xa_free", [[], []])
        prot_free = rst.setdefault("prot_free", [None])
        kt_free = rst.setdefault("kt_free", [[None, None], [None, None]])
        vtmp_free = rst.setdefault("vtmp_free", [None])
        pvt_free = rst.setdefault("pvt_free", [None])
        psum_free = [None]
        cs_free = rst.setdefault("cs_free", [None, None])
        cs_toks = rst.setdefault("cs_toks", [None, None])
        prot = rope["prot"]
        pvt = rope.get("pvt")
        for si, (xg, hf) in enumerate(hseq):
            par = si % 2
            xgp = xg % 2
            x16tok = None
            if hf == 0:
                _wait(SY, cs_free[xgp])
                ld = SY.dma_start(
                    out=cs_t[:, xgp * 1024:(xgp + 1) * 1024]
                    .rearrange("p (a t) -> p a t", a=2),
                    in_=cs[:, :, xg * 512:(xg + 1) * 512])
                cs_toks[xgp] = _inc(S["cs0" if xgp == 0 else "cs1"], ld, 16)
            if from_x16:
                _wait(SY, *xa_free[par])
                ld = SY.dma_start(
                    out=xa_sb[:, par * HDC * 512:(par + 1) * HDC * 512]
                    .rearrange("p (d t) -> p d t", d=HDC),
                    in_=x16_dram[:, xg, hf * HDC:(hf + 1) * HDC, :])
                xtok = _inc(S["xs0" if par == 0 else "xs1"], ld, 16)
            else:
                _wait(GE, *xa_free[par])
                ld = GE.dma_start(
                    out=xa_sb[:, par * HDC * 512:(par + 1) * HDC * 512]
                    .rearrange("p (d t) -> p d t", d=HDC),
                    in_=xw[:, xg, hf * HDC:(hf + 1) * HDC, :])
                xtok = _inc(S["xa0" if par == 0 else "xa1"], ld, 16)
            if write_x16:
                _wait(SY, xtok)
                st = SY.dma_start(
                    out=x16_dram[:, xg, hf * HDC:(hf + 1) * HDC, :],
                    in_=xa_sb[:, par * HDC * 512:(par + 1) * HDC * 512]
                    .rearrange("p (d t) -> p d t", d=HDC))
                x16tok = _inc(S["x16a" if par == 0 else "x16b"], st, 16)
            _wait(TE, xtok)
            if si == 0:
                _wait(TE, *first_units)
            if hf == 0:
                _wait(TE, psum_free[0])
            for ci in range(nprj):
                if si == 0:
                    _wait(TE, w_tok[ci] if isinstance(w_tok, list) else w_tok)
                for dd in range(HDC):
                    ins = TE.matmul(
                        psum[:, ci * 512:(ci + 1) * 512],
                        w_sb[:, (ci * NDC + hf * HDC + dd) * 128:
                             (ci * NDC + hf * HDC + dd) * 128 + 128],
                        xa_sb[:, (par * HDC + dd) * 512:(par * HDC + dd) * 512 + 512],
                        start=(hf == 0 and dd == 0),
                        stop=(hf == 1 and dd == HDC - 1), skip_group_check=True)
            pe_tok = _inc(S["pe"], ins)
            xa_free[par] = [pe_tok, x16tok] if write_x16 else [pe_tok]
            if hf != 1:
                continue
            pe_done = _tok(S["pe"])
            t0 = xg * 512
            last_act = None
            for ci in range(nprj):
                is_v = rope["is_v"](ci)
                if not is_v:
                    kpar = ci % 2
                    _wait(SE, pe_done, kt_free[kpar][0], kt_free[kpar][1])
                    atok = SW.activation(out=kt_sb[:, kpar * 512:(kpar + 1) * 512],
                                         in_=psum[:, ci * 512:(ci + 1) * 512],
                                         func=AF.Copy, scale=scale)
                    last_act = atok
                    _wait(TE, atok, prot_free[0])
                    ins = TE.matmul(prot[:], rt_sb[:],
                                    kt_sb[:, kpar * 512:(kpar + 1) * 512],
                                    start=True, stop=True)
                    rtok = _inc(S["pe"], ins)
                    kt_free[kpar][1] = rtok
                    _wait(VE, cs_toks[xgp], atok)
                    VW.tensor_mul(t1_sb[:], kt_sb[:, kpar * 512:(kpar + 1) * 512],
                                  cs_t[:, xgp * 1024:xgp * 1024 + 512])
                    _wait(VE, rtok)
                    VW.tensor_mul(t2_sb[:], prot[:],
                                  cs_t[:, xgp * 1024 + 512:xgp * 1024 + 1024])
                    dtok = VW.tensor_add(
                        out_buf[:, rope["dest"](ci) * T + t0:
                                rope["dest"](ci) * T + t0 + 512],
                        t1_sb[:], t2_sb[:])
                    kt_free[kpar][0] = dtok
                    prot_free[0] = dtok
                    cs_free[xgp] = dtok
                else:
                    _wait(SE, pe_done, vtmp_free[0])
                    atok = SW.activation(out=vtmp[:], in_=psum[:, ci * 512:(ci + 1) * 512],
                                         func=AF.Copy)
                    last_act = atok
                    _wait(TE, atok, pvt_free[0])
                    for tt in range(4):
                        ins = TE.transpose(pvt[:, tt * 128:(tt + 1) * 128],
                                           vtmp[:, tt * 128:(tt + 1) * 128], id_sb[:])
                    ttok = _inc(S["pe"], ins)
                    vtmp_free[0] = ttok
                    _wait(VE, ttok)
                    kb0 = t0 // 128
                    ci_v = rope["dest"](ci)
                    for tt in range(4):
                        cptok = VW.tensor_copy(
                            V_tm[:, (ci_v * c.NKB + kb0 + tt) * 128:
                                 (ci_v * c.NKB + kb0 + tt) * 128 + 128],
                            pvt[:, tt * 128:(tt + 1) * 128])
                    pvt_free[0] = cptok
            psum_free[0] = last_act

    wkv_sb = sb("wkv_sb", [128, 4 * NDC * 128], BF16, RB)
    wqh0 = sb("wqh0", [128, 4 * NDC * 128], BF16, RB + 4 * NDC * 128 * 2)
    wqh1 = sb("wqh1", [128, 4 * NDC * 128], BF16, RB)
    xa_sb = sb("xa_sb", [128, 2 * HDC * 512], BF16, RC)
    wq_toks = [None, None]
    ld = GE.dma_start(
        out=wqh0[:].rearrange("p (q d m) -> p q d m", q=4, d=NDC),
        in_=wq[:, 0:4, :, :])
    wq_toks[0] = _inc(S["wq0"], ld, 16)
    with (
        nc.psum_tensor("pkv", [128, 4 * 512], F32) as pkv,
        nc.psum_tensor("prot1", [128, 512], F32) as prot1,
        nc.psum_tensor("pvt1", [128, 1024], BF16) as pvt1,
    ):
        ld = GE.dma_start(
            out=wkv_sb[:, 0:2 * NDC * 128]
            .rearrange("p (cc d m) -> p cc d m", cc=2, d=NDC),
            in_=wkv[:, 0:2, :, :])
        wkv_tok_k = _inc(S["wld0"], ld, 16)
        ld = GE.dma_start(
            out=wkv_sb[:, 2 * NDC * 128:4 * NDC * 128]
            .rearrange("p (cc d m) -> p cc d m", cc=2, d=NDC),
            in_=wkv[:, 2:4, :, :])
        wkv_tok = _inc(S["wkv2"], ld, 16)
        proj_phase(K_sb, wkv_sb,
                   [wkv_tok_k, wkv_tok_k, wkv_tok, wkv_tok], pkv, 4,
                   dict(prot=prot1, pvt=pvt1,
                        is_v=lambda ci: ci >= 2, dest=lambda ci: ci % 2),
                   list(c_ld_tok), 0, 1.0, xa_sb, {}, write_x16=True)

    nc.all_engine_barrier()

    if cutoff == "p1a":
        SY.wait_ge(S["o_st0"][0], 0)
        return nc
    SY.wait_ge(S["x16a"][0], S["x16a"][1])
    SY.wait_ge(S["x16b"][0], S["x16b"][1])
    with (
        nc.psum_tensor("pq", [128, 4 * 512], F32) as pq,
        nc.psum_tensor("prot2", [128, 512], F32) as prot2,
    ):
        ld = GE.dma_start(
            out=wqh1[:].rearrange("p (q d m) -> p q d m", q=4, d=NDC),
            in_=wq[:, 4:8, :, :])
        wq_toks[1] = _inc(S["wq1"], ld, 16)
        rst = {}
        for half in range(2):
            proj_phase(Q_sb, (wqh0 if half == 0 else wqh1)[:],
                       wq_toks[half], pq, 4,
                       dict(prot=prot2, pvt=None,
                            is_v=lambda ci: False,
                            dest=lambda ci, _h=half: _h * 4 + ci),
                       [], 0, float(c.scaling), xa_sb, rst, from_x16=True)

    nc.all_engine_barrier()

    if cutoff == "p1b":
        SY.wait_ge(S["o_st0"][0], 0)
        return nc
    sumacc = sb("sumacc", [128, 2 * 512], BF16, RC)
    rb_sb = sb("rb_sb", [128, 2 * 512], F32, RC + 3 * KB)
    rinv = sb("rinv", [1, 2 * 512], F32, RC + 7 * KB)
    bias_h = sb("bias_h", [128, 2], F32, RC + 11 * KB)
    nrm = sb("nrm", [1, 64], F32, RC + 11 * KB + 64)
    sqt = sb("sqt", [128, 512], BF16, RC + 12 * KB)
    Ppt = sb("Ppt", [128, 4 * 512], BF16, RC + 13 * KB)

    with (
        nc.psum_tensor("pS", [128, 4 * 512], F32) as pS,
        nc.psum_tensor("pO", [128, 2 * 512], F32) as pO,
        nc.psum_tensor("pden", [128, 512], F32) as pden,
        nc.psum_tensor("psml", [128, 512], F32) as psml,
    ):
        sq_free = None
        psml_free = None
        for h in range(c.KVH):
            slot = 0
            for kind, idx in [("q", 4 * h + gq) for gq in range(4)] + [("k", h)]:
                buf = Q_sb if kind == "q" else K_sb
                for tgi in range(NT512):
                    _wait(SE, sq_free)
                    atok = SW.activation(out=sqt[:],
                                         in_=buf[:, idx * T + tgi * 512:
                                                 idx * T + tgi * 512 + 512],
                                         func=AF.Square)
                    _wait(TE, atok, psml_free)
                    ins = TE.matmul(psml[:1, :], ones128[:], sqt[:],
                                    start=True, stop=True, skip_group_check=True)
                    mtok = _inc(S["pe"], ins)
                    sq_free = mtok
                    _wait(VE, mtok)
                    psml_free = VW.reduce_max(nrm[:, slot:slot + 1], psml[:1, :],
                                              axis=mybir.AxisListType.X)
                    slot += 1
            nq = 4 * NT512
            VW.reduce_max(nrm[:, 60:61], nrm[:, 0:nq], axis=mybir.AxisListType.X)
            VW.reduce_max(nrm[:, 61:62], nrm[:, nq:slot], axis=mybir.AxisListType.X)
            vtok = VW.tensor_mul(nrm[:, 62:63], nrm[:, 60:61], nrm[:, 61:62])
            _wait(SE, vtok)
            SW.activation(out=nrm[:, 63:64], in_=nrm[:, 62:63], func=AF.Ln)
            atok = SW.activation(out=nrm[:, 62:63], in_=nrm[:, 63:64], func=AF.Exp,
                                 scale=0.5)
            _wait(VE, atok)
            vtok = VW.tensor_scalar_mul(nrm[:, 63:64], nrm[:, 62:63], -1.0)
            _wait(TE, vtok, psml_free)
            ins = TE.matmul(psml[:, 0:1], ones1f[:], nrm[:, 63:64],
                            start=True, stop=True, skip_group_check=True)
            mtok = _inc(S["pe"], ins)
            sq_free = mtok
            _wait(VE, mtok)
            psml_free = VW.tensor_copy(bias_h[:, h:h + 1], psml[:, 0:1])
        nc.all_engine_barrier()

        pS_free = [None] * 4
        PptAV_free = [None] * 4
        PptSum_free = [None] * 4
        pO_free = [None, None]
        sum_free = [None, None]
        pden_free = [None, None]
        rinv_free = [None, None]
        mtok_mask = None
        pend = [None]
        gctr = [0]

        def emit_epi1():
            if pend[0] is None or pend[0].get("s1"):
                return
            e = pend[0]
            e["s1"] = True
            epar_ = e["epar"]
            pd = pden if epar_ == 0 else psml
            _wait(TE, e["sum_tok"], pden_free[epar_])
            ins = TE.matmul(pd[:1, :], ones128[:],
                            sumacc[:, epar_ * 512:(epar_ + 1) * 512],
                            start=True, stop=True, skip_group_check=True)
            dtok = _inc(S["pe"], ins)
            sum_free[epar_] = dtok
            _wait(VE, dtok, rinv_free[epar_])
            e["rtok"] = VW.reciprocal(rinv[:, epar_ * 512:(epar_ + 1) * 512],
                                      pd[:1, :])

        def emit_epi2():
            if pend[0] is None:
                return
            emit_epi1()
            e = pend[0]
            pend[0] = None
            epar_ = e["epar"]
            qpar_ = e["qpar"]
            pd = pden if epar_ == 0 else psml
            _wait(TE, e["rtok"])
            ins = TE.matmul(pd[:, :], ones1f[:],
                            rinv[:, epar_ * 512:(epar_ + 1) * 512],
                            start=True, stop=True, skip_group_check=True)
            btok = _inc(S["pe"], ins)
            rinv_free[epar_] = btok
            _wait(VE, btok)
            pden_free[epar_] = VW.tensor_copy(
                rb_sb[:, epar_ * 512:(epar_ + 1) * 512], pd[:, :])
            _wait(VE, e["av_done"])
            pO_free[qpar_] = VW.tensor_mul(
                y_sb[:, e["i"] * T + e["qg"] * 512:e["i"] * T + (e["qg"] + 1) * 512],
                pO[:, qpar_ * 512:(qpar_ + 1) * 512],
                rb_sb[:, epar_ * 512:(epar_ + 1) * 512])

        for h in range(c.KVH):
            for g in range(4):
                i = 4 * h + g
                for qg in range(c.QGN):
                    qpar = qg % 2
                    epar = gctr[0] % 2
                    gctr[0] += 1
                    nkb = 4 * (qg + 1)
                    _wait(VE, sum_free[epar])
                    VW.memset(sumacc[:, epar * 512:(epar + 1) * 512], 0.0)
                    exp_toks = [None] * nkb
                    q0s = [max(0, kb * 128 - qg * 512) for kb in range(nkb)]

                    def emit_av(kb, _nkb=None, _qpar=None, _h=None, _exp=None):
                        par = kb % 4
                        q0 = q0s[kb]
                        _wait(TE, exp_toks[kb])
                        if kb == 0:
                            _wait(TE, pO_free[qpar])
                        ins = TE.matmul(
                            pO[:, qpar * 512 + q0:qpar * 512 + 512],
                            V_tm[:, (h * c.NKB + kb) * 128:
                                 (h * c.NKB + kb) * 128 + 128],
                            Ppt[:, par * 512 + q0:par * 512 + 512],
                            start=(kb == 0), stop=(kb == nkb - 1),
                            skip_group_check=True)
                        PptAV_free[par] = _inc(S["pe"], ins)

                    for kb in range(nkb):
                        par = kb % 4
                        q0 = q0s[kb]
                        _wait(TE, pS_free[par])
                        ins = TE.matmul(
                            pS[:, par * 512 + q0:par * 512 + 512],
                            K_sb[:, h * T + kb * 128:h * T + kb * 128 + 128],
                            Q_sb[:, i * T + qg * 512 + q0:i * T + (qg + 1) * 512],
                            start=True, stop=True, skip_group_check=True)
                        qk_tok = _inc(S["pe"], ins)
                        if kb == 1:
                            emit_epi1()
                        elif kb == 3:
                            emit_epi2()
                        diag = kb >= qg * 4
                        if diag:
                            d0 = kb * 128 - qg * 512
                            _wait(VE, qk_tok)
                            mtok_mask = VW.tensor_add(
                                pS[:, par * 512 + d0:par * 512 + d0 + 128],
                                pS[:, par * 512 + d0:par * 512 + d0 + 128],
                                mask_sb[:])
                        _wait(SE, mtok_mask if diag else qk_tok,
                              PptAV_free[par], PptSum_free[par])
                        exp_toks[kb] = SW.activation(
                            out=Ppt[:, par * 512 + q0:par * 512 + 512],
                            in_=pS[:, par * 512 + q0:par * 512 + 512],
                            func=AF.Exp, bias=bias_h[:, h:h + 1])
                        pS_free[par] = exp_toks[kb]
                        _wait(VE, exp_toks[kb])
                        sum_tok = VW.tensor_add(
                            sumacc[:, epar * 512 + q0:epar * 512 + 512],
                            sumacc[:, epar * 512 + q0:epar * 512 + 512],
                            Ppt[:, par * 512 + q0:par * 512 + 512])
                        PptSum_free[par] = sum_tok
                        if kb >= 3:
                            emit_av(kb - 3)
                    for kb in range(max(0, nkb - 3), nkb):
                        emit_av(kb)
                    av_done = _tok(S["pe"])
                    pend[0] = dict(i=i, qg=qg, qpar=qpar, epar=epar,
                                   sum_tok=sum_tok, av_done=av_done)
        emit_epi2()

    nc.all_engine_barrier()

    if cutoff == "p2":
        SY.wait_ge(S["o_st0"][0], 0)
        return nc
    wo_sb = sb("wo_sb", [128, 2 * QH * 128], BF16, RC)
    wo_f = sb("wo_f", [128, 2 * QH * 128], F32, RC + 4 * KB)
    stage = sb("stage", [128, 4 * 512], F32, RC + 12 * KB)
    PTS = min(512, TOWN)
    cc_toks = []
    with nc.psum_tensor("pm3", [128, 4 * 512], F32) as pm3:
        wo_free = [None, None]
        pm_free = [None] * 4
        stage_free = [None] * 4
        wof_free = [None, None]
        ftoks = [None] * NDC

        def wo_load(mm_):
            par_ = mm_ % 2
            _wait(SE, wof_free[par_])
            ld = SE.dma_start(
                out=wo_f[:, par_ * QH * 128:(par_ + 1) * QH * 128]
                .rearrange("p (q mm) -> p q mm", q=QH),
                in_=wo[mm_])
            ftoks[mm_] = _inc(S["wo0" if par_ == 0 else "wo1"], ld, 16)

        wo_load(0)
        if NDC > 1:
            wo_load(1)
        for m in range(NDC):
            par = m % 2
            _wait(SE, ftoks[m], wo_free[par])
            wtok = SW.activation(out=wo_sb[:, par * QH * 128:(par + 1) * QH * 128],
                                 in_=wo_f[:, par * QH * 128:(par + 1) * QH * 128],
                                 func=AF.Copy)
            wof_free[par] = wtok
            if m + 2 < NDC:
                wo_load(m + 2)
            for tgT in range(T // PTS):
                ppar = (m * (T // PTS) + tgT) % 4
                _wait(TE, wtok, pm_free[ppar])
                for qc in range(QH):
                    ins = TE.matmul(
                        pm3[:, ppar * 512:ppar * 512 + PTS],
                        wo_sb[:, (par * QH + qc) * 128:(par * QH + qc) * 128 + 128],
                        y_sb[:, qc * T + tgT * PTS:qc * T + (tgT + 1) * PTS],
                        start=(qc == 0), stop=(qc == QH - 1), skip_group_check=True)
                mmtok = _inc(S["pe"], ins)
                if tgT == T // PTS - 1:
                    wo_free[par] = mmtok
                _wait(VE, mmtok, stage_free[ppar])
                pm_free[ppar] = VW.tensor_copy(stage[:, ppar * 512:ppar * 512 + PTS],
                                               pm3[:, ppar * 512:ppar * 512 + PTS])
                _wait(SY, pm_free[ppar])
                r = (tgT * PTS) // TOWN
                cofs = (tgT * PTS) % TOWN
                band = m // (NDC // NBAND)
                mb = m % (NDC // NBAND)
                st = SY.dma_start(
                    out=part_drams[band][r * DB + mb * 128:r * DB + (mb + 1) * 128,
                                         cofs:cofs + PTS],
                    in_=stage[:, ppar * 512:ppar * 512 + PTS])
                stage_free[ppar] = _inc(S[f"p_st{ppar}"], st, 16)
            if (m + 1) % (NDC // NBAND) == 0:
                band = m // (NDC // NBAND)
                for _pi in range(4):
                    GE.wait_ge(S[f"p_st{_pi}"][0], S[f"p_st{_pi}"][1])
                cc = GE.collective_compute(
                    "ReduceScatter", OP.add,
                    replica_groups=[[0, 1, 2, 3], [4, 5, 6, 7]],
                    ins=[part_drams[band].opt()],
                    outs=[ao_drams[band].opt()],
                )
                cc_toks.append(_inc(S["cc"], cc))

    if cutoff == "p3a":
        for _pi in range(4):
            SY.wait_ge(S[f"p_st{_pi}"][0], S[f"p_st{_pi}"][1])
        return nc
    nc.all_engine_barrier()

    if cutoff == "rs":
        SY.wait_ge(S["o_st0"][0], 0)
        return nc
    vx1 = sb("vx1", [128, NDC * TOWN], BF16, RA)
    aob = sb("aob", [128, 2 * TOWN], F32, RC)
    xrb = sb("xrb", [128, 2 * TOWN], F32, RC + 8 * KB)
    mb_sb = sb("mb_sb", [128, 2 * 512], F32, RD)
    rb2_sb = sb("rb2_sb", [128, 2 * 512], F32, RD + 4 * KB)
    vsq = sb("vsq", [128, 512], F32, RD + 8 * KB)
    stat = sb("stat", [1, 8 * 512], F32, RE)
    vtmpf = sb("vtmpf", [128, 512], F32, RF)

    def ln_phase(read_v, stats_ps, bc_ps, write_out, out_stage):
        stats_done = _tok(S["pe"])
        for tg in range(NTG):
            _wait(VE, stats_done)
            VW.tensor_scalar_mul(stat[:, tg * 512:tg * 512 + TGS],
                                 stats_ps[:1, tg * 512:tg * 512 + TGS], 1.0 / D)
            VW.tensor_scalar_mul(stat[:, (2 + tg) * 512:(2 + tg) * 512 + TGS],
                                 stats_ps[:1, (2 + tg) * 512:(2 + tg) * 512 + TGS],
                                 1.0 / D)
            VW.tensor_mul(stat[:, (4 + tg) * 512:(4 + tg) * 512 + TGS],
                          stat[:, tg * 512:tg * 512 + TGS],
                          stat[:, tg * 512:tg * 512 + TGS])
            vtok = VW.tensor_sub(stat[:, (4 + tg) * 512:(4 + tg) * 512 + TGS],
                                 stat[:, (2 + tg) * 512:(2 + tg) * 512 + TGS],
                                 stat[:, (4 + tg) * 512:(4 + tg) * 512 + TGS])
            _wait(SE, vtok)
            SW.activation(out=stat[:, (6 + tg) * 512:(6 + tg) * 512 + TGS],
                          in_=stat[:, (4 + tg) * 512:(4 + tg) * 512 + TGS],
                          func=AF.Ln, bias=float(c.eps))
            atok = SW.activation(out=stat[:, (4 + tg) * 512:(4 + tg) * 512 + TGS],
                                 in_=stat[:, (6 + tg) * 512:(6 + tg) * 512 + TGS],
                                 func=AF.Exp, scale=-0.5)
            _wait(TE, atok)
            ins = TE.matmul(bc_ps[:, 0:TGS], ones1f[:],
                            stat[:, tg * 512:tg * 512 + TGS],
                            start=True, stop=True, skip_group_check=True)
            ttok = _inc(S["pe"], ins)
            _wait(VE, ttok)
            cptok = VW.tensor_copy(mb_sb[:, tg * 512:tg * 512 + TGS], bc_ps[:, 0:TGS])
            _wait(TE, cptok)
            ins = TE.matmul(bc_ps[:, 0:TGS], ones1f[:],
                            stat[:, (4 + tg) * 512:(4 + tg) * 512 + TGS],
                            start=True, stop=True, skip_group_check=True)
            ttok = _inc(S["pe"], ins)
            _wait(VE, ttok)
            VW.tensor_copy(rb2_sb[:, tg * 512:tg * 512 + TGS], bc_ps[:, 0:TGS])
        bc_done = _tok(S["dve"])
        return bc_done

    with (
        nc.psum_tensor("pst3", [128, 4 * 512], F32) as pst3,
        nc.psum_tensor("pbc3", [128, 512], F32) as pbc3,
    ):
        ao_free = [None, None]
        xr_free = [None, None]
        vtmpf_free = []
        for m in range(NDC):
            par = m % 2
            band = m // (NDC // NBAND)
            mb = m % (NDC // NBAND)
            _wait(SY, ao_free[par], cc_toks[band])
            st = SY.dma_start(out=aob[:, par * TOWN:(par + 1) * TOWN],
                              in_=ao_drams[band][mb * 128:(mb + 1) * 128, :])
            aotok = _inc(S["ao0" if par == 0 else "ao1"], st, 16)
            _wait(SY, xr_free[par])
            st = SY.dma_start(out=xrb[:, par * TOWN:(par + 1) * TOWN],
                              in_=xro[:, m, :])
            xrtok = _inc(S["xr0" if par == 0 else "xr1"], st, 16)
            for tg in range(NTG):
                _wait(VE, aotok, xrtok, *vtmpf_free)
                vtok = VW.tensor_add(vtmpf[:, :TGS],
                                     aob[:, par * TOWN + tg * TGS:
                                         par * TOWN + (tg + 1) * TGS],
                                     xrb[:, par * TOWN + tg * TGS:
                                         par * TOWN + (tg + 1) * TGS])
                _wait(SE, vtok, *vtmpf_free)
                sqtok = SW.activation(out=vsq[:, :TGS], in_=vtmpf[:, :TGS], func=AF.Square)
                _wait(TE, vtok)
                ins = TE.matmul(pst3[:1, tg * 512:tg * 512 + TGS], ones128f[:],
                                vtmpf[:, :TGS], start=(m == 0), stop=(m == NDC - 1),
                                skip_group_check=True)
                _inc(S["pe"], ins)
                _wait(TE, sqtok)
                ins = TE.matmul(pst3[:1, (2 + tg) * 512:(2 + tg) * 512 + TGS],
                                ones128f[:], vsq[:, :TGS],
                                start=(m == 0), stop=(m == NDC - 1),
                                skip_group_check=True)
                sttok = _inc(S["pe"], ins)
                vctok = VW.tensor_copy(
                    vx1[:, m * TOWN + tg * TGS:m * TOWN + (tg + 1) * TGS],
                    vtmpf[:, :TGS])
                vtmpf_free = [sqtok, sttok]
                if tg == NTG - 1:
                    ao_free[par] = vctok
                    xr_free[par] = vctok
        ln_phase(None, pst3, pbc3, None, None)
        for m in range(NDC):
            for tg in range(NTG):
                sl = slice(m * TOWN + tg * TGS, m * TOWN + (tg + 1) * TGS)
                VW.tensor_sub(vtmpf[:, :TGS], vx1[:, sl],
                              mb_sb[:, tg * 512:tg * 512 + TGS])
                VW.tensor_mul(vx1[:, sl], vtmpf[:, :TGS],
                              rb2_sb[:, tg * 512:tg * 512 + TGS])
        _wait(SY, _tok(S["dve"]))
        st = SY.dma_start(out=x1_dram,
                          in_=vx1[:].rearrange("p (d t) -> p d t", d=NDC))
        _inc(S["x1_st"], st, 16)

    nc.all_engine_barrier()

    if cutoff == "p3b":
        SY.wait_ge(S["o_st0"][0], 0)
        return nc
    acc1 = sb("acc1", [128, c.MH * TOWN], F32, RB)
    g_grp = sb("g_grp", [128, 8 * TOWN], BF16, RD)
    w13_sb = sb("w13_sb", [128, 2 * 2 * NDC * 128], BF16, RC)
    w2s_sb = sb("w2s_sb", [128, 2 * 8 * 4 * 128], BF16, RE)
    at_sb = sb("at_sb", [128, 2 * NDC * 16], BF16, RF)
    a2_sb = sb("a2_sb", [128, NJ * 16], BF16, RF + 2 * NDC * 16 * 2)
    xa1s = sb("xa1s", [16, TOWN], BF16, RF + 5 * KB)
    xa3s = sb("xa3s", [16, TOWN], BF16, RF + 5 * KB + 2 * TOWN)
    xa2s = sb("xa2s", [16, TOWN], BF16, RF + 5 * KB + 4 * TOWN)
    b1_sb = sb("b1_sb", [16, 2 * 128], BF16, RG + 2 * KB)
    b3_sb = sb("b3_sb", [16, 2 * 128], BF16, RG + 2 * KB + 512)
    sg1 = sb("sg1", [128, 512], BF16, RG + 3 * KB)

    _inc(S["misc"], GE.dma_start(
        out=at_sb[:, 0:NDC * 16].rearrange("p (d r) -> p d r", d=NDC), in_=a1t), 16)
    _inc(S["misc"], GE.dma_start(
        out=at_sb[:, NDC * 16:2 * NDC * 16].rearrange("p (d r) -> p d r", d=NDC),
        in_=a3t), 16)
    _inc(S["misc"], GE.dma_start(
        out=a2_sb[:].rearrange("p (j r) -> p j r", j=NJ), in_=a2t), 16)
    at_tok = _tok(S["misc"])

    with nc.psum_tensor("pxa", [16, 512], F32) as pxa:
        _wait(TE, at_tok)
        for wi, xout in [(0, xa1s), (1, xa3s)]:
            for tg in range(NTG):
                for dc in range(NDC):
                    ins = TE.matmul(
                        pxa[:, :TGS],
                        at_sb[:, wi * NDC * 16 + dc * 16:wi * NDC * 16 + (dc + 1) * 16],
                        vx1[:, dc * TOWN + tg * TGS:dc * TOWN + (tg + 1) * TGS],
                        start=(dc == 0), stop=(dc == NDC - 1), skip_group_check=True)
                xtok = _inc(S["pe"], ins)
                _wait(SE, xtok)
                atok = SW.activation(out=xout[:, tg * TGS:(tg + 1) * TGS],
                                     in_=pxa[:, :TGS],
                                     func=AF.Copy, scale=float(c.lora_scale))
                _wait(TE, atok)

    with (
        nc.psum_tensor("ph", [128, 4 * 512], F32) as ph,
        nc.psum_tensor("pw2", [128, 2 * 512], F32) as pw2,
        nc.psum_tensor("pxa2", [16, NTG * 512], F32) as pxa2,
    ):
        h_free = [None, None]
        w13_free = [None, None]
        b13_free = [None, None]
        pend_xa2 = []
        xa2_first = [True] * NTG
        sg1_free = None
        g_pe_free = None
        g_st_free = [None, None]
        pw2_free = [None, None]
        first_acc = [True] * (c.MH * NTG)
        w2s_free = [None, None]
        unit = 0
        nj_done = 0
        for gi, js in enumerate(c.G1):
            nj_g = len(js)
            for jj, j in enumerate(js):
                par = (nj_done + jj) % 2
                _wait(GE, w13_free[par])
                ld = GE.dma_start(
                    out=w13_sb[:, par * 2 * NDC * 128:par * 2 * NDC * 128 + NDC * 128]
                    .rearrange("p (d mm) -> p d mm", d=NDC), in_=w1b[j])
                _inc(S["w1a0" if par == 0 else "w1a1"], ld, 16)
                _wait(GE, b13_free[par])
                ld = GE.dma_start(out=b1_sb[:, par * 128:(par + 1) * 128], in_=b1t[j])
                wtok1 = _inc(S["w1a0" if par == 0 else "w1a1"], ld, 16)
                ld = GE.dma_start(
                    out=w13_sb[:, par * 2 * NDC * 128 + NDC * 128:
                               (par + 1) * 2 * NDC * 128]
                    .rearrange("p (d mm) -> p d mm", d=NDC), in_=w3b[j])
                _inc(S["w3a0" if par == 0 else "w3a1"], ld, 16)
                ld = GE.dma_start(out=b3_sb[:, par * 128:(par + 1) * 128], in_=b3t[j])
                wtok = _inc(S["w3a0" if par == 0 else "w3a1"], ld, 16)
                for tg in range(NTG):
                    upar = unit % 2
                    unit += 1
                    _wait(TE, wtok1, h_free[upar])
                    while pend_xa2:
                        j_, tg_, gsl_, gtok_ = pend_xa2.pop(0)
                        _wait(TE, gtok_)
                        ins = TE.matmul(pxa2[:, tg_ * 512:tg_ * 512 + TGS],
                                        a2_sb[:, j_ * 16:(j_ + 1) * 16],
                                        g_grp[:, gsl_],
                                        start=(xa2_first[tg_]), stop=(j_ == NJ - 1),
                                        skip_group_check=True)
                        xa2_first[tg_] = False
                        _inc(S["pe"], ins)
                    base = par * 2 * NDC * 128
                    for dc in range(NDC):
                        TE.matmul(
                            ph[:, upar * 1024:upar * 1024 + TGS],
                            w13_sb[:, base + dc * 128:base + dc * 128 + 128],
                            vx1[:, dc * TOWN + tg * TGS:dc * TOWN + (tg + 1) * TGS],
                            start=(dc == 0), stop=False, skip_group_check=True)
                    TE.matmul(ph[:, upar * 1024:upar * 1024 + TGS],
                              b1_sb[:, par * 128:(par + 1) * 128],
                              xa1s[:, tg * TGS:(tg + 1) * TGS],
                              start=False, stop=True, skip_group_check=True)
                    _wait(TE, wtok)
                    for dc in range(NDC):
                        TE.matmul(
                            ph[:, upar * 1024 + 512:upar * 1024 + 512 + TGS],
                            w13_sb[:, base + NDC * 128 + dc * 128:
                                   base + NDC * 128 + dc * 128 + 128],
                            vx1[:, dc * TOWN + tg * TGS:dc * TOWN + (tg + 1) * TGS],
                            start=(dc == 0), stop=False, skip_group_check=True)
                    ins = TE.matmul(ph[:, upar * 1024 + 512:upar * 1024 + 512 + TGS],
                                    b3_sb[:, par * 128:(par + 1) * 128],
                                    xa3s[:, tg * TGS:(tg + 1) * TGS],
                                    start=False, stop=True, skip_group_check=True)
                    htok = _inc(S["pe"], ins)
                    if tg == NTG - 1:
                        w13_free[par] = htok
                        b13_free[par] = htok
                    _wait(SE, htok, sg1_free)
                    atok = SW.activation(out=sg1[:, :TGS],
                                         in_=ph[:, upar * 1024:upar * 1024 + TGS],
                                         func=AF.Silu)
                    gsl = slice(jj * TOWN + tg * TGS, jj * TOWN + (tg + 1) * TGS)
                    _wait(VE, atok, g_pe_free, *g_st_free)
                    gtok = VW.tensor_mul(
                        g_grp[:, gsl], sg1[:, :TGS],
                        ph[:, upar * 1024 + 512:upar * 1024 + 512 + TGS])
                    h_free[upar] = gtok
                    sg1_free = gtok
                    pend_xa2.append((j, tg, gsl, gtok))
                _wait(SY, _tok(S["dve"]))
                st = SY.dma_start(out=g_dram[j],
                                  in_=g_grp[:, jj * TOWN:(jj + 1) * TOWN])
                _inc(S["g_st0" if jj % 2 == 0 else "g_st1"], st, 16)
            while pend_xa2:
                j_, tg_, gsl_, gtok_ = pend_xa2.pop(0)
                _wait(TE, gtok_)
                ins = TE.matmul(pxa2[:, tg_ * 512:tg_ * 512 + TGS],
                                a2_sb[:, j_ * 16:(j_ + 1) * 16],
                                g_grp[:, gsl_],
                                start=(xa2_first[tg_]), stop=(j_ == NJ - 1),
                                skip_group_check=True)
                xa2_first[tg_] = False
                _inc(S["pe"], ins)
            for mq in range(0, c.MH, 4):
                wpar = (mq // 4) % 2
                nm = min(4, c.MH - mq)
                _wait(GE, w2s_free[wpar])
                ld = GE.dma_start(
                    out=w2s_sb[:, wpar * 8 * 4 * 128:
                               wpar * 8 * 4 * 128 + nj_g * nm * 128]
                    .rearrange("p (j mx) -> p j mx", j=nj_g),
                    in_=w2b[js[0]:js[0] + nj_g, :, mq:mq + nm, :]
                    .rearrange("j p mm x -> p j (mm x)"))
                w2tok = _inc(S["w2ld0" if wpar == 0 else "w2ld1"], ld, 16)
                for mi in range(nm):
                    mcur = mq + mi
                    for tg in range(NTG):
                        ppar = (mcur * NTG + tg) % 2
                        _wait(TE, w2tok, pw2_free[ppar])
                        for jj in range(nj_g):
                            ins = TE.matmul(
                                pw2[:, ppar * 512:ppar * 512 + TGS],
                                w2s_sb[:, wpar * 8 * 4 * 128 + (jj * nm + mi) * 128:
                                       wpar * 8 * 4 * 128 + (jj * nm + mi) * 128 + 128],
                                g_grp[:, jj * TOWN + tg * TGS:
                                      jj * TOWN + (tg + 1) * TGS],
                                start=(jj == 0), stop=(jj == nj_g - 1),
                                skip_group_check=True)
                        mmtok = _inc(S["pe"], ins)
                        _wait(VE, mmtok)
                        asl = slice(mcur * TOWN + tg * TGS, mcur * TOWN + (tg + 1) * TGS)
                        if first_acc[mcur * NTG + tg]:
                            tk = VW.tensor_copy(acc1[:, asl],
                                                pw2[:, ppar * 512:ppar * 512 + TGS])
                            first_acc[mcur * NTG + tg] = False
                        else:
                            tk = VW.tensor_add(acc1[:, asl], acc1[:, asl],
                                               pw2[:, ppar * 512:ppar * 512 + TGS])
                        pw2_free[ppar] = tk
                w2s_free[wpar] = _tok(S["pe"])
            g_pe_free = _tok(S["pe"])
            g_st_free = [_tok(S["g_st0"]), _tok(S["g_st1"])]
            nj_done += nj_g
        xa2_done = _tok(S["pe"])
        for tg in range(NTG):
            _wait(SE, xa2_done)
            SW.activation(out=xa2s[:, tg * TGS:(tg + 1) * TGS],
                          in_=pxa2[:, tg * 512:tg * 512 + TGS],
                          func=AF.Copy, scale=float(c.lora_scale))

    nc.all_engine_barrier()

    if cutoff == "p4a":
        SY.wait_ge(S["o_st0"][0], 0)
        return nc
    acc2 = sb("acc2", [128, (NDC - c.MH) * TOWN], F32, RA)
    g2_sb = sb("g2_sb", [128, 2 * 4 * TOWN], BF16, RD)
    w2s2 = sb("w2s2", [128, 2 * 4 * 4 * 128], BF16, RE)
    b2_sb = sb("b2_sb", [16, NDC * 128], BF16, RC)

    with (
        nc.psum_tensor("pw2b", [128, 4 * 512], F32) as pw2,
        nc.psum_tensor("plora", [128, 2 * 512], F32) as plora,
    ):
        _inc(S["misc"], GE.dma_start(
            out=b2_sb[:].rearrange("p (d mm) -> p d mm", d=NDC), in_=b2t), 16)
        b2_tok = _tok(S["misc"])
        SY.wait_ge(S["g_st0"][0], S["g_st0"][1])
        SY.wait_ge(S["g_st1"][0], S["g_st1"][1])
        VE.wait_ge(S["x1_st"][0], S["x1_st"][1])
        g_free2 = [None, None]
        pw2_free = [None] * 4
        first2 = [True] * ((NDC - c.MH) * NTG)
        sched = []
        for gi, js in enumerate(c.G2):
            for mq in range(c.MH, NDC, 4):
                sched.append((gi, js, gi % 2, mq, min(4, NDC - mq)))
        w2chunk_free = [None, None]
        w2toks = [None] * len(sched)
        gtoks = [None] * len(c.G2)

        def emit_w2_dma(ci):
            gi, js, gpar, mq, nm = sched[ci]
            nj_g = len(js)
            wp = ci % 2
            if gtoks[gi] is None:
                _wait(SY, g_free2[gpar])
                ld = SY.dma_start(
                    out=g2_sb[:, gpar * 4 * TOWN:gpar * 4 * TOWN + nj_g * TOWN]
                    .rearrange("p (j t) -> p j t", j=nj_g),
                    in_=g_dram[js[0]:js[0] + nj_g].rearrange("j p t -> p j t"))
                gtoks[gi] = _inc(S["gl0" if gpar == 0 else "gl1"], ld, 16)
            _wait(GE, w2chunk_free[wp])
            ld = GE.dma_start(
                out=w2s2[:, wp * 4 * 4 * 128:wp * 4 * 4 * 128 + nj_g * nm * 128]
                .rearrange("p (j mx) -> p j mx", j=nj_g),
                in_=w2b[js[0]:js[0] + nj_g, :, mq:mq + nm, :]
                .rearrange("j p mm x -> p j (mm x)"))
            w2toks[ci] = _inc(S["w2ld0" if wp == 0 else "w2ld1"], ld, 16)

        emit_w2_dma(0)
        if len(sched) > 1:
            emit_w2_dma(1)
        for ci, (gi, js, gpar, mq, nm) in enumerate(sched):
            nj_g = len(js)
            wp = ci % 2
            for mi in range(nm):
                mcur = mq + mi
                m2 = mcur - c.MH
                for tg in range(NTG):
                    ppar = (m2 * NTG + tg) % 4
                    _wait(TE, w2toks[ci], gtoks[gi], pw2_free[ppar])
                    for jj in range(nj_g):
                        ins = TE.matmul(
                            pw2[:, ppar * 512:ppar * 512 + TGS],
                            w2s2[:, wp * 4 * 4 * 128 + (jj * nm + mi) * 128:
                                 wp * 4 * 4 * 128 + (jj * nm + mi) * 128 + 128],
                            g2_sb[:, gpar * 4 * TOWN + jj * TOWN + tg * TGS:
                                  gpar * 4 * TOWN + jj * TOWN + (tg + 1) * TGS],
                            start=(jj == 0), stop=(jj == nj_g - 1),
                            skip_group_check=True)
                    mmtok = _inc(S["pe"], ins)
                    _wait(VE, mmtok)
                    asl = slice(m2 * TOWN + tg * TGS, m2 * TOWN + (tg + 1) * TGS)
                    if first2[m2 * NTG + tg]:
                        tk = VW.tensor_copy(acc2[:, asl],
                                            pw2[:, ppar * 512:ppar * 512 + TGS])
                        first2[m2 * NTG + tg] = False
                    else:
                        tk = VW.tensor_add(acc2[:, asl], acc2[:, asl],
                                           pw2[:, ppar * 512:ppar * 512 + TGS])
                    pw2_free[ppar] = tk
            w2chunk_free[wp] = _tok(S["pe"])
            if ci == len(sched) - 1 or sched[ci + 1][0] != gi:
                g_free2[gpar] = _tok(S["pe"])
            if ci + 2 < len(sched):
                emit_w2_dma(ci + 2)
        plora_free = [None, None]
        for m in range(NDC):
            tgt = acc1 if m < c.MH else acc2
            mm2 = m if m < c.MH else m - c.MH
            for tg in range(NTG):
                ppar = (m * NTG + tg) % 2
                _wait(TE, b2_tok, plora_free[ppar])
                ins = TE.matmul(plora[:, ppar * 512:ppar * 512 + TGS],
                                b2_sb[:, m * 128:(m + 1) * 128],
                                xa2s[:, tg * TGS:(tg + 1) * TGS],
                                start=True, stop=True, skip_group_check=True)
                mtok = _inc(S["pe"], ins)
                _wait(VE, mtok)
                asl = slice(mm2 * TOWN + tg * TGS, mm2 * TOWN + (tg + 1) * TGS)
                plora_free[ppar] = VW.tensor_add(
                    tgt[:, asl], tgt[:, asl],
                    plora[:, ppar * 512:ppar * 512 + TGS])

    nc.all_engine_barrier()

    if cutoff == "p4b":
        SY.wait_ge(S["o_st0"][0], 0)
        return nc
    x1c = sb("x1c", [128, 2 * 512], BF16, RC)
    outb = sb("outb", [128, 2 * 512], F32, RC + 2 * KB)
    with (
        nc.psum_tensor("pst5", [128, 4 * 512], F32) as pst5,
        nc.psum_tensor("pb5", [128, 512], F32) as pb5,
    ):
        x1_free = [None, None]
        vsq_free = None
        SY.wait_ge(S["x1_st"][0], S["x1_st"][1])
        for m in range(NDC):
            tgt = acc1 if m < c.MH else acc2
            mm2 = m if m < c.MH else m - c.MH
            for tg in range(NTG):
                par = (m * NTG + tg) % 2
                _wait(SY, x1_free[par])
                st = SY.dma_start(out=x1c[:, par * 512:par * 512 + TGS],
                                  in_=x1_dram[:, m, tg * TGS:(tg + 1) * TGS])
                xtok = _inc(S["x1_ld0" if par == 0 else "x1_ld1"], st, 16)
                asl = slice(mm2 * TOWN + tg * TGS, mm2 * TOWN + (tg + 1) * TGS)
                _wait(VE, xtok)
                vtok = VW.tensor_add(tgt[:, asl], tgt[:, asl],
                                     x1c[:, par * 512:par * 512 + TGS])
                x1_free[par] = vtok
                _wait(SE, vtok, vsq_free)
                sqtok = SW.activation(out=vsq[:, :TGS], in_=tgt[:, asl],
                                      func=AF.Square)
                _wait(TE, vtok)
                ins = TE.matmul(pst5[:1, tg * 512:tg * 512 + TGS], ones128f[:],
                                tgt[:, asl], start=(m == 0), stop=(m == NDC - 1),
                                skip_group_check=True)
                _inc(S["pe"], ins)
                _wait(TE, sqtok)
                ins = TE.matmul(pst5[:1, (2 + tg) * 512:(2 + tg) * 512 + TGS],
                                ones128f[:], vsq[:, :TGS],
                                start=(m == 0), stop=(m == NDC - 1),
                                skip_group_check=True)
                vsq_free = _inc(S["pe"], ins)
        bc_done = ln_phase(None, pst5, pb5, None, None)
        ob_free = [None, None]
        for m in range(NDC):
            tgt = acc1 if m < c.MH else acc2
            mm2 = m if m < c.MH else m - c.MH
            for tg in range(NTG):
                par = (m * NTG + tg) % 2
                asl = slice(mm2 * TOWN + tg * TGS, mm2 * TOWN + (tg + 1) * TGS)
                _wait(VE, bc_done, ob_free[par])
                VW.tensor_sub(vtmpf[:, :TGS], tgt[:, asl],
                              mb_sb[:, tg * 512:tg * 512 + TGS])
                otok = VW.tensor_mul(outb[:, par * 512:par * 512 + TGS],
                                     vtmpf[:, :TGS],
                                     rb2_sb[:, tg * 512:tg * 512 + TGS])
                _wait(SY, otok)
                st = SY.dma_start(out=out_fm[:, m, tg * TGS:(tg + 1) * TGS],
                                  in_=outb[:, par * 512:par * 512 + TGS])
                ob_free[par] = _inc(S["o_st0" if par == 0 else "o_st1"], st, 16)
        SY.wait_ge(S["o_st0"][0], S["o_st0"][1])
        SY.wait_ge(S["o_st1"][0], S["o_st1"][1])

    return nc



def _c(x):
    return np.ascontiguousarray(x)


def make_core_inputs(c: Cfg, x, w_qkv, w_o, w1, w3, w2, A1, B1, A3, B3, A2, B2):
    D, T, NDC, NJ = c.D, c.T, c.NDC, c.NJ
    QS = c.NH * c.HD
    KVS = c.NKV * c.HD

    invf = 1.0 / c.theta ** (np.arange(0, c.HD, 2, dtype=np.float32) / c.HD)
    pos = np.arange(T, dtype=np.float32)
    fr = np.einsum("t,f->tf", pos, invf)
    emb = np.concatenate([fr, fr], axis=-1)
    cs = _c(np.stack([np.cos(emb).T, np.sin(emb).T], axis=1)
            .astype(ml_dtypes.bfloat16))

    mask = np.where(np.arange(128)[:, None] <= np.arange(128)[None, :],
                    0.0, -1e30).astype(np.float32)
    R = np.zeros((128, 128), np.float32)
    for ii in range(64):
        R[ii, ii + 64] = -1.0
        R[ii + 64, ii] = 1.0
    consts = np.zeros((128, 520), np.float32)
    consts[:, 0:128] = mask
    consts[:, 128:256] = R.T
    consts[:, 256:384] = np.eye(128, dtype=np.float32)
    consts[:, 384] = 1.0
    consts[:, 385] = 1.0
    consts[0, 386:514] = 1.0
    consts[:, 514] = np.float32(c.eps)
    consts[:, 515] = 0.5
    consts[:, 516] = -0.5
    consts[:, 517] = np.float32(c.scaling)
    consts[:, 518] = np.float32(c.lora_scale)
    consts[:, 519] = 1.0
    consts = _c(consts)

    w1bb = _c(w1.reshape(NJ, 128, NDC, 128).transpose(0, 3, 2, 1))
    w3bb = _c(w3.reshape(NJ, 128, NDC, 128).transpose(0, 3, 2, 1))
    w2bb = _c(w2.reshape(NDC, 128, NJ, 128).transpose(2, 3, 0, 1))
    a1tb = _c(A1.reshape(16, NDC, 128).transpose(2, 1, 0))
    a3tb = _c(A3.reshape(16, NDC, 128).transpose(2, 1, 0))
    a2tb = _c(A2.reshape(16, NJ, 128).transpose(2, 1, 0))
    b1tb = _c(B1.reshape(NJ, 128, 16).transpose(0, 2, 1))
    b3tb = _c(B3.reshape(NJ, 128, 16).transpose(0, 2, 1))
    b2tb = _c(B2.reshape(NDC, 128, 16).transpose(2, 0, 1))

    in_maps = []
    for core in range(8):
        b = core // 4
        g4 = core % 4
        xb = np.asarray(x[b])
        xwc = _c(xb.reshape(c.XGN, 512, NDC, 128).transpose(3, 0, 2, 1))
        xown = xb[g4 * c.TOWN:(g4 + 1) * c.TOWN]
        xroc = _c(xown.reshape(c.TOWN, NDC, 128).transpose(2, 1, 0))
        cols = []
        for hh in range(c.KVH):
            h0 = 2 * g4 + hh
            cols.append(w_qkv[:, QS + h0 * 128:QS + h0 * 128 + 128])
        for hh in range(c.KVH):
            h0 = 2 * g4 + hh
            cols.append(w_qkv[:, QS + KVS + h0 * 128:QS + KVS + h0 * 128 + 128])
        wkv_c = np.stack(cols, axis=1)
        wkv_c = _c(wkv_c.reshape(NDC, 128, 4, 128).transpose(1, 2, 0, 3))
        wq_c = w_qkv[:, c.QH * g4 * 128:(c.QH * g4 + c.QH) * 128]
        wq_c = _c(wq_c.reshape(NDC, 128, c.QH, 128).transpose(1, 2, 0, 3))
        wo_c = w_o[c.QH * g4 * 128:(c.QH * g4 + c.QH) * 128, :]
        wo_c = _c(wo_c.reshape(c.QH, 128, NDC, 128).transpose(2, 1, 0, 3))
        in_maps.append(dict(
            xw=xwc, xro=xroc, wkv=wkv_c, wq=wq_c, wo=wo_c, w1b=w1bb, w3b=w3bb,
            w2b=w2bb, a1t=a1tb, a3t=a3tb, a2t=a2tb, b1t=b1tb, b3t=b3tb, b2t=b2tb,
            cs=cs, consts=consts,
        ))
    return in_maps


def assemble_output(c: Cfg, results):
    out = np.empty((2, c.T, c.D), np.float32)
    for core in range(8):
        b = core // 4
        g4 = core % 4
        o = results[core]["out_fm"]
        out[b, g4 * c.TOWN:(g4 + 1) * c.TOWN, :] = (
            o.transpose(2, 1, 0).reshape(c.TOWN, c.D))
    return out


def kernel(L0_attn_hidden_states, L0_attn_w_qkv, L0_attn_w_o, L0_ffn_w1, L0_ffn_w3,
           L0_ffn_w2, L0_ffn_lora_A1, L0_ffn_lora_B1, L0_ffn_lora_A3, L0_ffn_lora_B3,
           L0_ffn_lora_A2, L0_ffn_lora_B2):
    c = Cfg()
    in_maps = make_core_inputs(
        c, np.asarray(L0_attn_hidden_states), np.asarray(L0_attn_w_qkv),
        np.asarray(L0_attn_w_o), np.asarray(L0_ffn_w1), np.asarray(L0_ffn_w3),
        np.asarray(L0_ffn_w2), np.asarray(L0_ffn_lora_A1), np.asarray(L0_ffn_lora_B1),
        np.asarray(L0_ffn_lora_A3), np.asarray(L0_ffn_lora_B3),
        np.asarray(L0_ffn_lora_A2), np.asarray(L0_ffn_lora_B2))
    nc = build_nc(c)
    res = run_bass_kernel_spmd(nc, in_maps, core_ids=list(range(8)))
    return assemble_output(c, res.results)
